# version 37
# baseline (speedup 1.0000x reference)
import numpy as np
import ml_dtypes
import concourse.bass as bass
import concourse.mybir as mybir
from concourse.bass_utils import run_bass_kernel_spmd
from contextlib import ExitStack

F32 = mybir.dt.float32
F32R = mybir.dt.float32r
BF16 = mybir.dt.bfloat16
AF = mybir.ActivationFunctionType
ALU = mybir.AluOpType
AX = mybir.AxisListType

NSS = 8
ENGS = ("sync", "gpsimd", "vector", "scalar", "tensor")
SEMS = ("LD", "LPW", "LX", "LV", "LWa", "LWb", "LWc", "LWd", "L2a", "L2b",
        "L2c", "L2d", "SA", "SB", "SC", "SSY", "SSC", "ST", "PE", "DV", "AC",
        "PL", "PD")


class Prog:

    def __init__(self):
        self.q = {e: [] for e in ENGS}
        self.cnt = {s: 0 for s in SEMS}

    def op(self, eng, fn, inc=None, wait=()):
        w = tuple((s, v) for (s, v) in wait if v > 0)
        self.q[eng].append(("op", fn, w, inc))
        if inc is not None:
            self.cnt[inc[0]] += inc[1]
            return self.cnt[inc[0]]
        return None

    def drain(self, eng):
        self.q[eng].append(("drain", None, (), None))

    def wait(self, eng, sem, val):
        if val > 0:
            self.q[eng].append(("wait", None, ((sem, val),), None))

    def emit(self, block, sems):
        def mk(entries):
            def _(e):
                for kind, fn, waits, inc in entries:
                    for s, v in waits:
                        e.wait_ge(sems[s], v)
                    if kind == "drain":
                        e.drain()
                    elif kind == "op":
                        ins = fn(e)
                        if inc is not None:
                            ins.then_inc(sems[inc[0]], inc[1])
            return _
        block.sync(mk(self.q["sync"]))
        block.gpsimd(mk(self.q["gpsimd"]))
        block.vector(mk(self.q["vector"]))
        block.scalar(mk(self.q["scalar"]))
        block.tensor(mk(self.q["tensor"]))


S2ALL = frozenset(["b2", "qvtr", "evac", "mem", "attd", "membuf", "num",
                   "rr", "bc", "u", "fin"])


def build_program(stop="F", c_lim=16, s2p=S2ALL):
    _ph = {"prep": 0, "A": 1, "B": 2, "C": 3, "D": 4, "E": 5, "F": 6}[stop]
    nc = bass.Bass("TRN2", target_bir_lowering=False)
    x_d = nc.dram_tensor("x", [4096, 1024], F32, kind="ExternalInput").ap()
    wq_d = nc.dram_tensor("wq", [1024, 1024], F32, kind="ExternalInput").ap()
    wk_d = nc.dram_tensor("wk", [1024, 1024], F32, kind="ExternalInput").ap()
    wv_d = nc.dram_tensor("wv", [1024, 1024], F32, kind="ExternalInput").ap()
    wo_d = nc.dram_tensor("wo", [1024, 1024], F32, kind="ExternalInput").ap()
    bet_d = nc.dram_tensor("betas", [4, 64], F32, kind="ExternalInput").ap()
    w1_d = nc.dram_tensor("w1", [4096, 1024], F32, kind="ExternalInput").ap()
    b1_d = nc.dram_tensor("b1", [4096], F32, kind="ExternalInput").ap()
    w2_d = nc.dram_tensor("w2", [1024, 4096], F32, kind="ExternalInput").ap()
    b2_d = nc.dram_tensor("b2", [1024], F32, kind="ExternalInput").ap()
    y_d = nc.dram_tensor("y", [4096, 1024], F32, kind="ExternalOutput").ap()
    w1t_d = nc.dram_tensor("w1t_s", [32, 128, 1024], BF16, kind="Internal").ap()
    w2t_d = nc.dram_tensor("w2t_s", [32, 128, 1024], BF16, kind="Internal").ap()
    wvo_d = nc.dram_tensor("wvo_s", [2, 128, 8192], BF16, kind="Internal").ap()

    p = Prog()

    with ExitStack() as ctx:
        E = ctx.enter_context
        ident = E(nc.sbuf_tensor("ident", [128, 128], BF16))
        identf = E(nc.sbuf_tensor("identf", [64, 64], F32))
        onesf = E(nc.sbuf_tensor("onesf", [1, 128], F32))
        onesc = E(nc.sbuf_tensor("onesc", [128, 1], F32))
        wq_sb = E(nc.sbuf_tensor("wq_sb", [128, 8192], BF16))
        wk_sb = E(nc.sbuf_tensor("wk_sb", [128, 8192], BF16))
        wvo_sb = E(nc.sbuf_tensor("wvo_sb", [128, 8192], BF16))
        b1c = E(nc.sbuf_tensor("b1c", [128, 32], F32))
        b2bc = E(nc.sbuf_tensor("b2bc", [128, 1024], F32))
        g4 = E(nc.sbuf_tensor("g4", [64, 4], F32))
        omg4 = E(nc.sbuf_tensor("omg4", [64, 4], F32))
        gz4 = E(nc.sbuf_tensor("gz4", [64, 8], F32))

        zc_pref = E(nc.sbuf_tensor("zc_pref", [64, 4], F32))

        sems = {}
        for s in SEMS:
            sems[s] = E(nc.semaphore(s))
        pA = E(nc.psum_tensor("pA", [128, 512], F32))
        pB = E(nc.psum_tensor("pB", [128, 512], F32))
        pC = E(nc.psum_tensor("pC", [128, 512], F32))
        pD = E(nc.psum_tensor("pD", [128, 512], F32))
        pS0 = E(nc.psum_tensor("pS0", [128, 512], F32))
        pS1 = E(nc.psum_tensor("pS1", [128, 512], F32))
        pAT = E(nc.psum_tensor("pAT", [128, 512], F32))
        pTR = E(nc.psum_tensor("pTR", [128, 1024], BF16))
        block = E(nc.Block())

        with ExitStack() as pctx:
            PP = pctx.enter_context
            wstage = PP(nc.sbuf_tensor("wstage", [128, 9216], F32))
            sgn1 = PP(nc.sbuf_tensor("sgn1", [128, 2048], F32))
            sgnm = PP(nc.sbuf_tensor("sgnm", [128, 2048], F32))
            wtc = PP(nc.sbuf_tensor("wtc", [128, 2048], BF16))
            wasm = PP(nc.sbuf_tensor("wasm", [128, 8192], BF16))
            wabs = PP(nc.sbuf_tensor("wabs", [128, 32], F32))
            thrc = PP(nc.sbuf_tensor("thrc", [128, 2], F32))
            trow = PP(nc.sbuf_tensor("trow", [1, 8], F32))
            b1row = PP(nc.sbuf_tensor("b1row", [1, 4096], F32))
            graw = PP(nc.sbuf_tensor("graw", [64, 4], F32))

            p.op("gpsimd", lambda g: g.iota(ident[:, :], [[1, 128]], channel_multiplier=-1,
                                            allow_small_or_imprecise_dtypes=True))
            p.drain("gpsimd")
            p.op("gpsimd", lambda g: g.tensor_scalar(out=ident[:, :], in0=ident[:, :],
                                                     scalar1=0.0, scalar2=None, op0=ALU.is_equal))
            p.op("gpsimd", lambda g: g.iota(identf[:, :], [[1, 64]], channel_multiplier=-1,
                                            allow_small_or_imprecise_dtypes=True))
            p.drain("gpsimd")
            p.op("gpsimd", lambda g: g.tensor_scalar(out=identf[:, :], in0=identf[:, :],
                                                     scalar1=0.0, scalar2=None, op0=ALU.is_equal))
            p.op("gpsimd", lambda g: g.memset(onesf[:, :], 1.0), inc=("PL", 1))
            p.op("gpsimd", lambda g: g.memset(onesc[:, :], 1.0))
            p.op("gpsimd", lambda g: g.memset(zc_pref[:, :], 0.0), inc=("PL", 1))
            pl_init = p.cnt["PL"]

            p.op("sync", lambda s: s.dma_start(out=graw[:, :], in_=bet_d.rearrange("h j -> j h"),
                                               allow_slow_non_contiguous=True), inc=("LD", 16))
            p.op("sync", lambda s: s.dma_start(out=b1row[0:1, :],
                                               in_=b1_d.rearrange("(a b) -> a b", a=1)), inc=("LD", 16))
            ld0 = p.op("sync", lambda s: s.dma_start(out=b2bc[0:1, :],
                                                     in_=b2_d.rearrange("(a b) -> a b", a=1)), inc=("LD", 16))
            ac0 = p.op("scalar", lambda s: s.activation(out=g4[:, :], in_=graw[:, :], func=AF.Sigmoid),
                       wait=[("LD", ld0)], inc=("AC", 1))
            p.op("vector", lambda v: v.tensor_scalar(out=omg4[:, :], in0=g4[:, :], scalar1=-1.0,
                                                     scalar2=1.0, op0=ALU.mult, op1=ALU.add),
                 wait=[("AC", ac0)], inc=("DV", 1))

            peb = 0
            for F in range(32):
                peb = p.op("tensor", lambda t, F=F: t.transpose(
                    pA[:, F:F + 1], b1row[0:1, 128 * F:128 * F + 128], onesf[0:1, 0:1]),
                    wait=([("LD", ld0), ("PL", pl_init)] if F == 0 else ()), inc=("PE", 1))
            acb1 = p.op("scalar", lambda s: s.copy(out=b1c[:, :], in_=pA[:, 0:32]),
                        wait=[("PE", peb)], inc=("AC", 1))
            for h in range(2):
                peb = p.op("tensor", lambda t, h=h: t.matmul(
                    pB[:, 0:512], onesf[0:1, 0:128], b2bc[0:1, 512 * h:512 * h + 512],
                    start=True, stop=True),
                    wait=[("AC", acb1 if h == 0 else p.cnt["AC"])], inc=("PE", 1))
                p.op("scalar", lambda s, h=h: s.copy(out=b2bc[:, 512 * h:512 * h + 512],
                                                     in_=pB[:, 0:512]),
                     wait=[("PE", peb)], inc=("AC", 1))

            QENG = ("sync", "scalar", "gpsimd")
            SCH = ("LPW", "SA", "L2a", "LX", "SB", "L2b", "LV", "SC", "L2c")
            NSLOT = 9
            slot_gate = [[] for _ in range(NSLOT)]
            stq = [0]

            def emit_load(slot, src, gates):
                ch = SCH[slot]
                p.op(QENG[slot % 3], lambda e, src=src, slot=slot: e.dma_start(
                    out=wstage[:, 1024 * slot:1024 * slot + 1024], in_=src),
                    wait=gates, inc=(ch, 16))

            def emit_store(dst, src, waits):
                sch = ("SSY", "SSC")[stq[0] % 2]
                eng = ("sync", "scalar")[stq[0] % 2]
                stq[0] += 1
                p.op(eng, lambda e, dst=dst, src=src: e.dma_start(out=dst, in_=src),
                     wait=waits, inc=(sch, 16))

            def run_pipe(jobs):
                n = len(jobs)
                for i in range(min(NSLOT, n)):
                    jobs[i][0](i % NSLOT, slot_gate[i % NSLOT])
                for i in range(n):
                    g = jobs[i][1](i % NSLOT)
                    slot_gate[i % NSLOT] = g
                    if i + NSLOT < n:
                        jobs[i + NSLOT][0](i % NSLOT, g)

            def mk_cast(wd, mi, dc):
                def load(slot, gates):
                    emit_load(slot, wd[128 * dc:128 * dc + 128, :], gates)

                def consume(slot):
                    ch = SCH[slot]
                    if mi < 2:
                        wsb = wq_sb if mi == 0 else wk_sb
                        v = p.op("vector", lambda ve, wsb=wsb, dc=dc, slot=slot: ve.tensor_copy(
                            out=wsb[:, 1024 * dc:1024 * dc + 1024],
                            in_=wstage[:, 1024 * slot:1024 * slot + 1024]),
                            wait=[(ch, p.cnt[ch])], inc=("DV", 1))
                        return [("DV", v)]
                    w = [(ch, p.cnt[ch])]
                    if mi == 3 and dc == 0:
                        w += [("SSY", p.cnt["SSY"]), ("SSC", p.cnt["SSC"])]
                    v = p.op("vector", lambda ve, dc=dc, slot=slot: ve.tensor_copy(
                        out=wvo_sb[:, 1024 * dc:1024 * dc + 1024],
                        in_=wstage[:, 1024 * slot:1024 * slot + 1024]),
                        wait=w, inc=("DV", 1))
                    emit_store(wvo_d[mi - 2][:, 1024 * dc:1024 * dc + 1024],
                               wvo_sb[:, 1024 * dc:1024 * dc + 1024], [("DV", v)])
                    return [("DV", v)]
                return (load, consume)

            jobs = []
            for mi, wd in enumerate([wq_d, wk_d, wv_d, wo_d]):
                for dc in range(8):
                    jobs.append(mk_cast(wd, mi, dc))
            run_pipe(jobs)

            for wi in range(2):
                if wi == 0:
                    def srcw(c):
                        return w1_d[128 * c:128 * c + 128, :]
                else:
                    def srcw(c):
                        return w2_d[128 * (c % 8):128 * (c % 8) + 128,
                                    1024 * (c // 8):1024 * (c // 8) + 1024]

                def mk_stat(c, srcw=srcw):
                    def load(slot, gates):
                        emit_load(slot, srcw(c), gates)

                    def consume(slot):
                        ch = SCH[slot]
                        v = p.op("vector", lambda e, c=c, slot=slot: e.tensor_reduce(
                            out=wabs[:, c:c + 1],
                            in_=wstage[:, 1024 * slot:1024 * slot + 1024],
                            axis=AX.X, op=ALU.add, apply_absolute_value=True),
                            wait=[(ch, p.cnt[ch])], inc=("DV", 1))
                        return [("DV", v)]
                    return (load, consume)

                run_pipe([mk_stat(c) for c in range(32)])
                p.drain("vector")
                dvs = p.op("vector", lambda v: v.tensor_reduce(
                    out=wabs[:, 0:1], in_=wabs[:, 0:32], axis=AX.X, op=ALU.add),
                    wait=[("PL", p.cnt["PL"])], inc=("DV", 1))
                pet = p.op("tensor", lambda t: t.matmul(pA[0:1, 0:1], wabs[:, 0:1], onesc[:, 0:1],
                                                        start=True, stop=True),
                           wait=[("DV", dvs), ("AC", p.cnt["AC"])], inc=("PE", 1))
                p.op("vector", lambda v: v.tensor_scalar(
                    out=trow[0:1, 0:1], in0=pA[0:1, 0:1], scalar1=1.0 / (4096.0 * 1024.0),
                    scalar2=1e-8, op0=ALU.mult, op1=ALU.max), wait=[("PE", pet)])
                p.drain("vector")
                p.op("vector", lambda v: v.tensor_scalar(
                    out=trow[0:1, 1:2], in0=trow[0:1, 0:1], scalar1=0.5, scalar2=None, op0=ALU.mult))
                p.drain("vector")
                dvs = p.op("vector", lambda v: v.tensor_scalar(
                    out=trow[0:1, 2:3], in0=trow[0:1, 1:2], scalar1=-1.0, scalar2=None,
                    op0=ALU.mult), inc=("DV", 1))
                pet = p.op("tensor", lambda t: t.matmul(pA[:, 0:2], onesf[0:1, 0:128],
                                                        trow[0:1, 1:3], start=True, stop=True),
                           wait=[("DV", dvs), ("AC", p.cnt["AC"])], inc=("PE", 1))
                acthr = p.op("scalar", lambda s: s.copy(out=thrc[:, :], in_=pA[:, 0:2]),
                             wait=[("PE", pet)], inc=("AC", 1))

                st8 = {"stt": [0, 0], "tr": [0, 0], "evac": 0}

                def mk_xf(c, srcw=srcw, wi=wi, acthr=acthr, st8=st8):
                    def load(slot, gates):
                        emit_load(slot, srcw(c), gates)

                    def consume(slot):
                        ch = SCH[slot]
                        par = c % 2
                        w0 = [(ch, p.cnt[ch])]
                        if c == 0:
                            w0 = w0 + [("AC", acthr)]
                        wp = [("DV", st8["stt"][par])] if st8["stt"][par] else []
                        a1 = p.op("scalar", lambda s, par=par, slot=slot: s.activation(
                            out=sgn1[:, 1024 * par:1024 * par + 1024],
                            in_=wstage[:, 1024 * slot:1024 * slot + 1024],
                            func=AF.Sign, bias=thrc[:, 1:2]), wait=w0 + wp, inc=("AC", 1))
                        dvm = p.op("vector", lambda v, par=par, slot=slot: v.tensor_scalar(
                            out=sgnm[:, 1024 * par:1024 * par + 1024],
                            in0=wstage[:, 1024 * slot:1024 * slot + 1024],
                            scalar1=thrc[:, 1:2], scalar2=2.0,
                            op0=ALU.is_lt, op1=ALU.mult), wait=w0 + wp, inc=("DV", 1))
                        p.drain("vector")
                        wt = [("AC", a1)]
                        if st8["tr"][par]:
                            wt.append(("PE", st8["tr"][par]))
                        st8["stt"][par] = p.op("vector", lambda v, par=par: v.scalar_tensor_tensor(
                            out=wtc[:, 1024 * par:1024 * par + 1024],
                            in0=sgn1[:, 1024 * par:1024 * par + 1024], scalar=1.0,
                            in1=sgnm[:, 1024 * par:1024 * par + 1024],
                            op0=ALU.add, op1=ALU.subtract), wait=wt, inc=("DV", 1))
                        pel = 0
                        for k in range(8):
                            w = ()
                            if k == 0:
                                w = [("DV", st8["stt"][par]), ("AC", st8["evac"])]
                            pel = p.op("tensor", lambda t, k=k, par=par: t.transpose(
                                pTR[:, 128 * k:128 * k + 128],
                                wtc[:, 1024 * par + 128 * k:1024 * par + 128 * k + 128],
                                ident[:, :]), wait=w, inc=("PE", 1))
                        st8["tr"][par] = pel
                        if wi == 0:
                            sch = ("SSY", "SSC")[par]
                            acw = p.op("scalar", lambda s, par=par: s.copy(
                                out=wasm[:, 1024 * par:1024 * par + 1024], in_=pTR[:, 0:1024]),
                                wait=[("PE", pel), (sch, p.cnt[sch])], inc=("AC", 1))
                            st8["evac"] = acw
                            p.op(("sync", "scalar")[par], lambda e, c=c, par=par: e.dma_start(
                                out=w1t_d[c], in_=wasm[:, 1024 * par:1024 * par + 1024]),
                                wait=[("AC", acw)], inc=(sch, 16))
                        else:
                            j, Dc = divmod(c, 8)
                            wv_ = [("PE", pel)]
                            if Dc == 0:
                                wv_ += [("SSY", p.cnt["SSY"]), ("SSC", p.cnt["SSC"])]
                            dst = wasm[:, :].rearrange("p (k c) -> p k c", k=8)[:, :, 128 * Dc:128 * Dc + 128]
                            acw = p.op("scalar", lambda s, dst=dst: s.copy(
                                out=dst, in_=pTR[:, :].rearrange("p (k c) -> p k c", k=8)),
                                wait=wv_, inc=("AC", 1))
                            st8["evac"] = acw
                            if Dc == 7:
                                for k in range(8):
                                    emit_store(w2t_d[8 * j + k],
                                               wasm[:, 1024 * k:1024 * k + 1024],
                                               [("AC", acw)])
                        return [("AC", a1), ("DV", dvm)]
                    return (load, consume)

                run_pipe([mk_xf(c) for c in range(32)])

            p.wait("gpsimd", "SSY", p.cnt["SSY"])
            p.wait("gpsimd", "SSC", p.cnt["SSC"])
            for eng in ("gpsimd", "vector", "scalar", "tensor"):
                p.drain(eng)
                p.op(eng, lambda e: e.nop(), inc=("PD", 1))
            p.op("sync", lambda e: e.nop(), inc=("PD", 1))
            for eng in ENGS:
                p.wait(eng, "PD", 5)

        T1 = E(nc.sbuf_tensor("T1", [128, 8192], BF16))
        T2 = E(nc.sbuf_tensor("T2", [128, 8192], BF16))
        xT = E(nc.sbuf_tensor("xT", [128, 4096], BF16))
        yT = E(nc.sbuf_tensor("yT", [128, 4096], BF16))
        xs = E(nc.sbuf_tensor("xs", [128, 2048], F32))
        lnp = E(nc.sbuf_tensor("lnp", [128, 2048], BF16))
        stats = E(nc.sbuf_tensor("stats", [128, 16], F32))
        y1 = E(nc.sbuf_tensor("y1", [128, 4096], F32))
        Pt = E(nc.sbuf_tensor("Pt", [128, 4096], BF16))
        vaug = E(nc.sbuf_tensor("vaug", [128, 520], BF16))
        sqtm = E(nc.sbuf_tensor("sqtm", [128, 512], BF16))

        zaccs = E(nc.sbuf_tensor("zaccs", [64, 8], F32))
        znt = E(nc.sbuf_tensor("znt", [64, 8], F32))
        zrec = E(nc.sbuf_tensor("zrec", [64, 16], F32))
        membuf = E(nc.sbuf_tensor("membuf", [64, 130], BF16))
        ATT = E(nc.sbuf_tensor("ATT", [128, 4096], BF16))
        HH = E(nc.sbuf_tensor("HH", [128, 16384], BF16))
        w1s = E(nc.sbuf_tensor("w1s", [128, 4096], BF16))
        w2s = E(nc.sbuf_tensor("w2s", [128, 2048], BF16))
        outst = E(nc.sbuf_tensor("outst", [64, 2048], F32))
        ssq = lnp
        SQV = HH[:, 0:8192]
        SQ = HH[0:64, 0:8192]
        zsc = xs[0:64, 0:2048]
        mem_pref = T1[64:128, 4232:4744].bitcast(F32)
        rr_in = T1[64:65, 4744:6792].bitcast(F32)
        rrow = E(nc.sbuf_tensor("rrow", [1, 1024], F32R))
        gblkZ = E(nc.sbuf_tensor("gblkZ", [1, 1024], F32R))
        bcB = y1[:, 3072:4096]
        uB = outst[0:64, 0:1024]
        uA = outst[0:64, 1024:2048]

        p.op("gpsimd", lambda g: g.memset(
            vaug[:, :].rearrange("p (k c) -> p k c", c=65)[:, :, 64:65], 1.0), inc=("PL", 1))
        p.op("gpsimd", lambda g: g.memset(mem_pref[:, :], 0.0), inc=("PL", 1))
        p.op("gpsimd", lambda g: g.memset(
            membuf[:, :].rearrange("p (k c) -> p k c", c=65)[:, :, 64:65], 1.0), inc=("PL", 1))
        p.op("gpsimd", lambda g: g.memset(rr_in[0:1, 0:1024], 0.0), inc=("PL", 1))
        pl_ms = p.cnt["PL"]
        dv_z = p.op("vector", lambda v: v.tensor_copy(out=gblkZ[:, :],
                                                      in_=rr_in[0:1, 0:1024]),
                    wait=[("PL", pl_ms)], inc=("DV", 1))
        pe_gb = 0
        for k in range(8):
            src = omg4[:, k:k + 1] if k < 4 else g4[:, k - 4:k - 4 + 1]
            pe_gb = p.op("tensor", lambda t, k=k, src=src: t.transpose(
                pC[0:1, 64 * k:64 * k + 64], src, identf[0:64, 0:64]),
                wait=([("DV", p.cnt["DV"]), ("AC", p.cnt["AC"])] if k == 0 else ()),
                inc=("PE", 1))
        ac_gb = p.op("scalar", lambda s: s.copy(out=rr_in[0:1, 0:512], in_=pC[0:1, 0:512]),
                     wait=[("PE", pe_gb), ("DV", dv_z)], inc=("AC", 1))
        dv_gb = 0
        for ho in range(4):
            p.op("vector", lambda v, ho=ho: v.tensor_copy(
                out=gblkZ[0:1, 128 * ho:128 * ho + 64],
                in_=rr_in[0:1, 256 + 64 * ho:256 + 64 * ho + 64]),
                wait=([("AC", ac_gb), ("PL", pl_ms)] if ho == 0 else ()), inc=("DV", 1))
            dv_gb = p.op("vector", lambda v, ho=ho: v.tensor_copy(
                out=gblkZ[0:1, 512 + 128 * ho + 64:512 + 128 * ho + 128],
                in_=rr_in[0:1, 64 * ho:64 * ho + 64]), inc=("DV", 1))
        p.wait("tensor", "PL", p.cnt["PL"])

        QT = T1[0:64, :]
        VT = SQV[64:128, :]
        KT = T2[0:64, :]
        QTr = T1[0:64, :].rearrange("i (hs r m) -> i hs r m", r=16, m=32)
        VTr = SQV[64:128, :].rearrange("i (hs r m) -> i hs r m", r=16, m=32)
        KTr = T2[0:64, :].rearrange("i (hs r m) -> i hs r m", r=16, m=32)

        M = {
            "vev": 0, "sqev": 0, "memadd": 0, "memlast": 0, "numpe": 0,
            "bcpe": 0, "bcbe": 0, "u1_0": 0, "u1_1": 0, "u2": 0, "rrB": 0, "lnr": 0,
            "expR": 0, "ek_last": 0, "sq_s": [0, 0, 0, 0], "gz_s": [0, 0, 0, 0],
            "atr_evac": 0, "ln_tr": [0, 0, 0, 0], "peB0": 0,
            "ln_ap": [0, 0, 0, 0], "ln_sq": [0, 0, 0, 0], "ln_red": [0, 0, 0, 0],
            "resid": [0, 0, 0, 0], "expq": [0, 0, 0],
            "h1": [0, 0], "gelu": [0, 0],
            "ffn_o3": 0, "epi0": [0, 0, 0, 0], "epi1": [0, 0, 0, 0],
            "fstop": [0, 0, 0, 0], "wo_last": 0,
            "ac_xT": 0, "ac_yT": 0,
        }

        def newton(dst, srcv, scr):
            p.drain("vector")
            p.op("vector", lambda v: v.reciprocal(out=dst, in_=srcv))
            p.drain("vector")
            p.op("vector", lambda v: v.tensor_mul(out=scr, in0=srcv, in1=dst))
            p.drain("vector")
            p.op("vector", lambda v: v.tensor_scalar(out=scr, in0=scr, scalar1=2.0,
                                                     scalar2=-1.0, op0=ALU.subtract,
                                                     op1=ALU.mult))
            p.drain("vector")
            p.op("vector", lambda v: v.tensor_mul(out=dst, in0=dst, in1=scr),
                 inc=("DV", 1))
            p.drain("vector")

        def ln_stats(s, src_ap, src_wait):
            par = s % 2
            dvr = p.op("vector", lambda v: v.tensor_reduce(
                out=stats[:, s:s + 1], in_=src_ap, axis=AX.X, op=ALU.add),
                wait=src_wait, inc=("DV", 1))
            M["ln_red"][s] = dvr
            p.drain("scalar")
            M["ln_sq"][s] = p.op("scalar", lambda sc, par=par: sc.activation(
                out=ssq[:, 1024 * par:1024 * par + 1024], in_=src_ap, func=AF.Square,
                accum_out=stats[:, 4 + s:5 + s]),
                wait=list(src_wait) + [("PE", M["ln_tr"][(s + 2) % 4])], inc=("AC", 1))
            p.drain("vector")
            p.op("vector", lambda v: v.tensor_scalar(
                out=stats[:, s:s + 1], in0=stats[:, s:s + 1], scalar1=1.0 / 1024.0,
                scalar2=None, op0=ALU.mult))
            p.op("vector", lambda v: v.tensor_scalar(
                out=stats[:, 4 + s:5 + s], in0=stats[:, 4 + s:5 + s], scalar1=1.0 / 1024.0,
                scalar2=None, op0=ALU.mult), wait=[("AC", M["ln_sq"][s])])
            p.drain("vector")
            p.op("vector", lambda v: v.tensor_mul(
                out=stats[:, 12 + s:13 + s], in0=stats[:, s:s + 1], in1=stats[:, s:s + 1]))
            p.drain("vector")
            dvv = p.op("vector", lambda v: v.tensor_scalar(
                out=stats[:, 4 + s:5 + s], in0=stats[:, 4 + s:5 + s],
                scalar1=stats[:, 12 + s:13 + s], scalar2=1e-5,
                op0=ALU.subtract, op1=ALU.add), inc=("DV", 1))
            p.op("scalar", lambda sc: sc.activation(
                out=stats[:, 4 + s:5 + s], in_=stats[:, 4 + s:5 + s], func=AF.Ln),
                wait=[("DV", dvv)])
            p.drain("scalar")
            acs = p.op("scalar", lambda sc: sc.activation(
                out=stats[:, 8 + s:9 + s], in_=stats[:, 4 + s:5 + s],
                func=AF.Exp, scale=-0.5), inc=("AC", 1))
            aw = [("PE", M["ln_tr"][(s + 2) % 4]), ("AC", acs)]
            M["ln_ap"][s] = p.op("vector", lambda v: v.tensor_scalar(
                out=lnp[:, 1024 * par:1024 * par + 1024], in0=src_ap,
                scalar1=stats[:, s:s + 1], scalar2=stats[:, 8 + s:9 + s],
                op0=ALU.subtract, op1=ALU.mult), wait=aw, inc=("DV", 1))

        def ln_trans(s, dstT):
            par = s % 2
            pel = 0
            for dc in range(8):
                w = ()
                if dc == 0:
                    w = [("DV", max(M["ln_ap"][s], M["vev"], M["sqev"])),
                         ("AC", M["atr_evac"])]
                pel = p.op("tensor", lambda t, dc=dc, par=par: t.transpose(
                    pTR[:, 128 * dc:128 * dc + 128],
                    lnp[:, 1024 * par + 128 * dc:1024 * par + 128 * dc + 128],
                    ident[:, :]), wait=w, inc=("PE", 1))
            M["ln_tr"][s] = pel
            dstv = dstT[:, :].rearrange("p (dc c) -> p dc c", c=512)[:, :, 128 * s:128 * s + 128]
            M["atr_evac"] = p.op("scalar", lambda sc, dstv=dstv: sc.copy(
                out=dstv, in_=pTR[:, :].rearrange("p (dc c) -> p dc c", dc=8)),
                wait=[("PE", pel)], inc=("AC", 1))

        def ldx_op(ss, s, w):
            return p.op("sync", lambda sy, ss=ss, s=s: sy.dma_start(
                out=xs[:, 1024 * (s % 2):1024 * (s % 2) + 1024],
                in_=x_d[512 * ss + 128 * s:512 * ss + 128 * s + 128, :]),
                wait=w, inc=("LX", 16))

        def phaseA_part(ss, half):
            if half == 0:
                w0 = [("DV", M["resid"][3])] if ss > 0 else []
                ldx_op(ss, 0, w0)
                ldx_op(ss, 1, w0)
                for s in (0, 1):
                    ln_stats(s, xs[:, 1024 * (s % 2):1024 * (s % 2) + 1024],
                             [("LX", p.cnt["LX"])])
            else:
                for s in (2, 3):
                    ldx_op(ss, s, [("DV", M["ln_red"][s - 1]), ("AC", M["ln_sq"][s - 1])])
                    ln_stats(s, xs[:, 1024 * (s % 2):1024 * (s % 2) + 1024],
                             [("LX", p.cnt["LX"])])

        phaseA_part(0, 0)
        for s in (0, 1):
            ln_trans(s, xT)
        phaseA_part(0, 1)
        for s in (2, 3):
            ln_trans(s, xT)
        M["ac_xT"] = p.cnt["AC"]

        for ss in range(NSS):
            if _ph < 1:
                break
            M["peB0"] = p.cnt["PE"]
            pl_B = p.cnt["PL"]
            grp = 0
            ev_val = [0, 0, 0]
            ev_eng = ["DV", "AC", "DV"]
            for wsb, tgt in ((wq_sb, QTr), (wk_sb, KTr), (wvo_sb, VTr)):
                if tgt is VTr:
                    p.op("sync", lambda sy: sy.dma_start(
                        out=wvo_sb[:, :], in_=wvo_d[0]),
                        wait=([("PE", M["wo_last"])] if M["wo_last"] else []),
                        inc=("LV", 16))
                for F in range(8):
                    bank = (pA, pB, pD)[grp % 3]
                    w0 = [("AC", M["ac_xT"])]
                    if tgt is VTr and F == 0:
                        w0.append(("LV", p.cnt["LV"]))
                    if grp == 2 and ss > 0:
                        w0.append(("DV", M["epi1"][3]))
                    if ev_val[grp % 3]:
                        w0.append((ev_eng[grp % 3], ev_val[grp % 3]))
                    pes = 0
                    for dc in range(8):
                        pes = p.op("tensor", lambda t, wsb=wsb, F=F, dc=dc, bank=bank: t.matmul(
                            bank[:, :], wsb[:, 1024 * dc + 128 * F:1024 * dc + 128 * F + 128],
                            xT[:, 512 * dc:512 * dc + 512],
                            start=(dc == 0), stop=(dc == 7)),
                            wait=(w0 if dc == 0 else ()),
                            inc=(("PE", 1) if dc == 7 else None))
                    eng = "vector" if grp % 2 == 0 else "scalar"
                    last = 0
                    for pr in range(2):
                        r = 2 * F + pr
                        dst = tgt[:, :, r, :]
                        wev = [("PE", pes)] if pr == 0 else []
                        if grp == 0 and F == 0 and pr == 0:
                            wev.append(("PL", pl_B))
                        if eng == "vector":
                            last = p.op("vector", lambda v, dst=dst, pr=pr, bank=bank: v.tensor_copy(
                                out=dst, in_=bank[64 * pr:64 * pr + 64, :]),
                                wait=wev, inc=("DV", 1))
                        else:
                            last = p.op("scalar", lambda sc, dst=dst, pr=pr, bank=bank: sc.copy(
                                out=dst, in_=bank[64 * pr:64 * pr + 64, :]),
                                wait=([("PE", pes)] if pr == 0 else ()), inc=("AC", 1))
                    ev_val[grp % 3] = last
                    ev_eng[grp % 3] = "DV" if eng == "vector" else "AC"
                    grp += 1
            p.op("sync", lambda sy: sy.dma_start(
                out=wvo_sb[:, :], in_=wvo_d[1]),
                wait=[("PE", p.cnt["PE"])], inc=("LV", 16))
            evB_dv = p.cnt["DV"]
            evB_ac = p.cnt["AC"]
            dv_f = p.cnt["DV"]
            st_B = p.cnt["ST"]

            _phC = _ph >= 3

            def b2_sq(s):
                B2 = 2048 * s
                wmin = [("DV", M["ek_last"])]
                if s == 0:
                    wmin += [("DV", evB_dv), ("AC", evB_ac)]
                m1 = p.op("vector", lambda v, B2=B2: v.tensor_scalar(
                    out=zsc[:, 0:2048], in0=QT[:, B2:B2 + 2048],
                    scalar1=0.0, scalar2=None, op0=ALU.min), wait=wmin, inc=("DV", 1))
                e1 = p.op("scalar", lambda sc: sc.activation(
                    out=zsc[:, 0:2048], in_=zsc[:, 0:2048], func=AF.Exp),
                    wait=[("DV", m1)], inc=("AC", 1))
                dsq = p.op("vector", lambda v, B2=B2: v.scalar_tensor_tensor(
                    out=SQ[:, B2:B2 + 2048], in0=QT[:, B2:B2 + 2048], scalar=0.0,
                    in1=zsc[:, 0:2048], op0=ALU.max, op1=ALU.add),
                    wait=[("AC", e1), ("PE", M["peB0"])], inc=("DV", 1))
                M["sq_s"][s] = dsq

            def b2_ek(s):
                B2 = 2048 * s
                m2 = p.op("vector", lambda v, B2=B2: v.tensor_scalar(
                    out=zsc[:, 0:2048], in0=KT[:, B2:B2 + 2048],
                    scalar1=0.0, scalar2=None, op0=ALU.min),
                    wait=[("DV", M["sq_s"][s])], inc=("DV", 1))
                e2 = p.op("scalar", lambda sc: sc.activation(
                    out=zsc[:, 0:2048], in_=zsc[:, 0:2048], func=AF.Exp),
                    wait=[("DV", m2)], inc=("AC", 1))
                for h in range(4):
                    w = [("AC", e2), ("AC", M["ln_sq"][2])] if h == 0 else []
                    M["ek_last"] = p.op("vector", lambda v, s=s, h=h: v.scalar_tensor_tensor(
                        out=zsc[:, 512 * h:512 * h + 512],
                        in0=KT[:, (4 * s + h) * 512:(4 * s + h) * 512 + 512],
                        scalar=0.0, in1=zsc[:, 512 * h:512 * h + 512],
                        op0=ALU.max, op1=ALU.add, accum_out=znt[:, h:h + 1]),
                        wait=w, inc=("DV", 1))
                p.drain("vector")
                p.op("vector", lambda v: v.tensor_add(out=zc_pref[:, :], in0=zc_pref[:, :],
                                                      in1=znt[:, 0:4]))
                p.drain("vector")
                newton(zrec[:, 4 * s:4 * s + 4], zc_pref[:, 0:4], zaccs[:, 0:4])
                M["gz_s"][s] = p.cnt["DV"]

            def stage1(idx):
                s, ho = divmod(idx, 4)
                par = idx % 2
                Bq = (4 * s + ho) * 512
                ncs = 4
                if s2p is not S2ALL and "sc1" in s2p:
                    ncs = 1
                elif s2p is not S2ALL and "sc2" in s2p:
                    ncs = 2
                elif s2p is not S2ALL and "sc3" in s2p:
                    ncs = 3
                for c in (range(ncs) if ("sc" in s2p or "sc1" in s2p or "sc2" in s2p
                                         or "sc3" in s2p or s2p is S2ALL) else []):
                    bank = (pS0, pS1)[(4 * idx + c) % 2]
                    w = [("AC", M["expq"][-2])]
                    if c == 0 and "nw" not in s2p:
                        w.append(("DV", dv_f))
                    if idx == 0:
                        w += [("DV", dv_f), ("AC", evB_ac)]
                    spe = p.op("tensor", lambda t, c=c, Bq=Bq, bank=bank: t.matmul(
                        bank[:, :], KT[:, Bq + 128 * c:Bq + 128 * c + 128],
                        QT[:, Bq:Bq + 512], start=True, stop=True),
                        wait=w, inc=("PE", 1))
                    if "sx" not in s2p and s2p is not S2ALL:
                        M["expq"].append(0)
                        M["pexp_%d_%d" % (idx, c)] = 0
                        continue
                    w2 = [("PE", spe)]
                    if c == 0:
                        w2.append(("PE", M.get("attd_%d" % (idx - 2), 0)))
                    a = p.op("scalar", lambda sc, c=c, par=par, bank=bank: sc.activation(
                        out=Pt[:, 2048 * par + 512 * c:2048 * par + 512 * c + 512],
                        in_=bank[:, :], func=AF.Exp, scale=0.125), wait=w2, inc=("AC", 1))
                    M["expq"].append(a)
                    M["pexp_%d_%d" % (idx, c)] = a
                M["expq"] = M["expq"][-8:]

            def stage2(idx):
                s, ho = divmod(idx, 4)
                par = idx % 2
                Bq = (4 * s + ho) * 512
                qtr = 0
                for c in (range(4) if "qvtr" in s2p else []):
                    w = ()
                    if c == 0:
                        w = [("DV", max(M["sqev"], M["vev"], M["sq_s"][s]))]
                        if idx == 0:
                            w += [("DV", evB_dv), ("AC", evB_ac)]
                    qtr = p.op("tensor", lambda t, c=c, Bq=Bq: t.transpose(
                        pTR[:, 128 * c:128 * c + 128],
                        SQV[:, Bq + 128 * c:Bq + 128 * c + 128],
                        ident[:, :]), wait=w, inc=(("PE", 1) if c == 3 else None))
                sqv_src = pTR[:, 0:512].rearrange("p (k c) -> p k c", c=128)
                if "evac" not in s2p:
                    return
                M["sqev"] = p.op("vector", lambda v, par=par, sqv_src=sqv_src: v.tensor_copy(
                    out=sqtm[:, 256 * par:256 * par + 256].rearrange("p (k c) -> p k c", c=64),
                    in_=sqv_src[:, :, 0:64]),
                    wait=[("PE", qtr), ("PE", M.get("mem_%d" % (idx - 2), 0))], inc=("DV", 1))
                vdst = vaug[:, 260 * par:260 * par + 260].rearrange(
                    "p (k c) -> p k c", c=65)[:, :, 0:64]
                M["vev"] = p.op("vector", lambda v, vdst=vdst, sqv_src=sqv_src: v.tensor_copy(
                    out=vdst, in_=sqv_src[:, :, 64:128]),
                    wait=[("PE", max(M.get("attd_%d" % (idx - 2), 0),
                                     M.get("mem_%d" % (idx - 2), 0)))], inc=("DV", 1))
                M["vev_%d" % idx] = M["vev"]
                if "mem" not in s2p:
                    return
                for c in range(4):
                    w = [("DV", max(M["sqev"], M["vev"], M["memadd"]))]
                    p.op("tensor", lambda t, c=c, par=par: t.matmul(
                        pA[64:128, 0:64], sqtm[:, 256 * par + 64 * c:256 * par + 64 * c + 64],
                        vaug[:, 260 * par + 65 * c:260 * par + 65 * c + 64],
                        start=(c == 0), stop=(c == 3)),
                        wait=(w if c == 0 else ()), inc=(("PE", 1) if c == 3 else None))
                M["memlast"] = p.cnt["PE"]
                M["mem_%d" % idx] = M["memlast"]
                if "attd" not in s2p:
                    return
                abank = pAT if par == 0 else pD
                for c in range(4):
                    w = [("AC", M["pexp_%d_%d" % (idx, c)]), ("DV", M["vev_%d" % idx])]
                    if c == 0:
                        w.append(("DV", max(M["u1_%d" % par], dv_f)))
                    p.op("tensor", lambda t, c=c, par=par, abank=abank: t.matmul(
                        abank[0:65, :], vaug[:, 260 * par + 65 * c:260 * par + 65 * c + 65],
                        Pt[:, 2048 * par + 512 * c:2048 * par + 512 * c + 512],
                        start=(c == 0), stop=(c == 3)),
                        wait=w, inc=(("PE", 1) if c == 3 else None))
                M["attd_%d" % idx] = p.cnt["PE"]
                if "membuf" not in s2p:
                    return
                M["memadd"] = p.op("vector", lambda v, ho=ho: v.tensor_add(
                    out=mem_pref[:, 64 * ho:64 * ho + 64],
                    in0=mem_pref[:, 64 * ho:64 * ho + 64], in1=pA[64:128, 0:64]),
                    wait=[("PE", M["mem_%d" % idx])], inc=("DV", 1))
                mb = p.op("gpsimd", lambda g, ho=ho, par=par: g.tensor_copy(
                    out=membuf[:, 65 * par:65 * par + 64],
                    in_=mem_pref[:, 64 * ho:64 * ho + 64]),
                    wait=[("DV", M["memadd"]),
                          ("PE", M.get("num_%d" % (idx - 2), 0))], inc=("PL", 1))
                if "num" not in s2p:
                    return
                penum = p.op("tensor", lambda t, par=par, Bq=Bq: t.matmul(
                    pC[0:65, :], membuf[:, 65 * par:65 * par + 65],
                    SQ[:, Bq:Bq + 512], start=True, stop=True),
                    wait=[("PL", mb), ("DV", max(M["u2"], dv_f))], inc=("PE", 1))
                M["numpe"] = penum
                M["num_%d" % idx] = penum
                if "rr" not in s2p:
                    return
                p.op("vector", lambda v, abank=abank: v.tensor_copy(
                    out=rr_in[0:1, 0:512], in_=abank[64:65, :]),
                    wait=[("PE", M["attd_%d" % idx]), ("AC", M["lnr"])], inc=("DV", 1))
                M["rrB"] = p.op("vector", lambda v: v.tensor_copy(
                    out=rr_in[0:1, 512:1024], in_=pC[64:65, :]),
                    wait=[("PE", penum)], inc=("DV", 1))
                M["lnr"] = p.op("scalar", lambda sc: sc.activation(
                    out=rr_in[0:1, 0:1024], in_=rr_in[0:1, 0:1024], func=AF.Ln),
                    wait=[("DV", M["rrB"])], inc=("AC", 1))
                p.drain("scalar")
                we = [("PE", M["bcpe"])]
                if idx == 0:
                    we.append(("DV", dv_f))
                M["expR"] = p.op("scalar", lambda sc: sc.activation(
                    out=rrow[0:1, 0:1024], in_=rr_in[0:1, 0:1024], func=AF.Exp,
                    scale=-1.0), wait=we, inc=("AC", 1))
                if "bc" not in s2p:
                    return
                w = [("AC", M["expR"]), ("DV", M["bcbe"])]
                if idx == 0:
                    w += [("DV", dv_f), ("AC", evB_ac)]
                p.op("tensor", lambda t, ho=ho: t.matmul(
                    pB[:, :], gblkZ[0:1, 128 * ho:128 * ho + 128], rrow[0:1, 512:1024],
                    start=True, stop=False), wait=w, inc=("PE", 1))
                M["bcpe"] = p.op("tensor", lambda t, ho=ho: t.matmul(
                    pB[:, :], gblkZ[0:1, 512 + 128 * ho:512 + 128 * ho + 128],
                    rrow[0:1, 0:512], start=False, stop=True), inc=("PE", 1))
                wb = [("PE", M["bcpe"])]
                if idx == 0:
                    wb.append(("ST", st_B))
                M["bcbe"] = p.op("vector", lambda v, par=par: v.tensor_copy(
                    out=bcB[:, 512 * par:512 * par + 512], in_=pB[:, :]),
                    wait=wb, inc=("DV", 1))
                if "u" not in s2p:
                    return
                M["u1_%d" % par] = p.op("vector", lambda v, par=par, abank=abank: v.tensor_mul(
                    out=uA[:, 512 * par:512 * par + 512], in0=abank[0:64, :],
                    in1=bcB[64:128, 512 * par:512 * par + 512]),
                    wait=[("PL", M.get("fin_%d" % (idx - 2), 0))], inc=("DV", 1))
                M["u2"] = p.op("vector", lambda v, par=par, s=s, ho=ho: v.scalar_tensor_tensor(
                    out=uB[:, 512 * par:512 * par + 512], in0=pC[0:64, :],
                    scalar=zrec[:, 4 * s + ho:4 * s + ho + 1],
                    in1=bcB[0:64, 512 * par:512 * par + 512],
                    op0=ALU.mult, op1=ALU.mult), inc=("DV", 1))
                if "fin" not in s2p:
                    return
                attv = ATT[:, :].rearrange("p (G c) -> p G c", c=512)
                fin = 0
                for pr in range(2):
                    uBv = uB[:, 512 * par:512 * par + 512].rearrange(
                        "j (G two m) -> j G two m", two=2, m=32)[:, :, pr, :]
                    uAv = uA[:, 512 * par:512 * par + 512].rearrange(
                        "j (G two m) -> j G two m", two=2, m=32)[:, :, pr, :]
                    dst = attv[64 * pr:64 * pr + 64, :,
                               128 * s + 32 * ho:128 * s + 32 * ho + 32]
                    w = [("DV", M["u2"])] if pr == 0 else ()
                    fin = p.op("gpsimd", lambda g, uBv=uBv, uAv=uAv, dst=dst:
                               g.tensor_tensor(out=dst, in0=uBv, in1=uAv, op=ALU.add),
                               wait=w, inc=("PL", 1))
                M["fin_%d" % idx] = fin

            for idx in (range(min(17, c_lim + 1)) if _phC else []):
                if idx >= 1:
                    stage2(idx - 1)
                if idx < min(16, c_lim):
                    if "b2" in s2p or s2p is S2ALL:
                        if idx == 0:
                            b2_sq(0)
                            b2_ek(0)
                        elif idx % 4 == 2 and idx < 14:
                            b2_sq(idx // 4 + 1)
                        elif idx % 4 == 3 and idx < 15:
                            b2_ek(idx // 4 + 1)
                    stage1(idx)

            if _ph < 4:
                continue
            for s in range(2):
                w = [("DV", M["ln_ap"][3]), ("AC", M["ln_sq"][3])]
                w.append(("DV", M["ek_last"]))
                p.op("sync", lambda sy, ss=ss, s=s: sy.dma_start(
                    out=xs[:, 1024 * (s % 2):1024 * (s % 2) + 1024],
                    in_=x_d[512 * ss + 128 * s:512 * ss + 128 * s + 128, :]),
                    wait=w, inc=("LX", 16))
            for s in range(4):
                for Dh in range(2):
                    k = 2 * s + Dh
                    bank = pA if k % 2 == 0 else pB
                    w = [("PL", M["fin_%d" % (4 * s + 3)])]
                    if k == 0:
                        w.append(("DV", M["memadd"]))
                        w.append(("LV", p.cnt["LV"]))
                    if k == 1:
                        w.append(("DV", M["bcbe"]))
                    if k >= 2:
                        w.append(("DV", M["woev_%d" % (k - 2)]))
                    pes = 0
                    for G in range(8):
                        pes = p.op("tensor", lambda t, s=s, Dh=Dh, G=G, bank=bank: t.matmul(
                            bank[:, :], ATT[:, (4 * G + s) * 128:(4 * G + s) * 128 + 128],
                            wvo_sb[:, 1024 * G + 512 * Dh:1024 * G + 512 * Dh + 512],
                            start=(G == 0), stop=(G == 7)),
                            wait=(w if G == 0 else ()), inc=(("PE", 1) if G == 7 else None))
                    M["wo_last"] = pes
                    dvr = p.op("vector", lambda v, s=s, Dh=Dh, bank=bank: v.tensor_add(
                        out=y1[:, 1024 * s + 512 * Dh:1024 * s + 512 * Dh + 512],
                        in0=bank[:, :],
                        in1=xs[:, 1024 * (s % 2) + 512 * Dh:1024 * (s % 2) + 512 * Dh + 512]),
                        wait=([("PE", pes), ("LX", p.cnt["LX"])]
                              + ([("ST", p.cnt["ST"])] if k == 0 else [])),
                        inc=("DV", 1))
                    M["woev_%d" % k] = dvr
                    if Dh == 1:
                        M["resid"][s] = dvr
                        if s < 2:
                            p.op("sync", lambda sy, ss=ss, s=s: sy.dma_start(
                                out=xs[:, 1024 * (s % 2):1024 * (s % 2) + 1024],
                                in_=x_d[512 * ss + 128 * (s + 2):512 * ss + 128 * (s + 2) + 128, :]),
                                wait=[("DV", dvr)], inc=("LX", 16))
                        if _ph >= 5:
                            if s >= 2:
                                ln_trans(s - 2, yT)
                            ln_stats(s, y1[:, 1024 * s:1024 * s + 1024],
                                     [("DV", M["resid"][s])])
            p.drain("vector")

            if _ph < 5:
                continue
            for s in (2, 3):
                ln_trans(s, yT)
            M["ac_yT"] = p.cnt["AC"]
            for s in range(4):
                w = [("AC", M["ac_yT"])] if s == 0 else ()
                p.op("vector", lambda v, s=s: v.tensor_add(
                    out=y1[:, 1024 * s:1024 * s + 1024], in0=y1[:, 1024 * s:1024 * s + 1024],
                    in1=b2bc[:, :]), wait=w, inc=(("DV", 1) if s == 3 else None))
            dv_y1b = p.cnt["DV"]
            p.drain("vector")
            if ss + 1 < NSS:
                phaseA_part(ss + 1, 0)

            if _ph < 6:
                continue
            banks = [pS0, pS1, pAT, pD]
            prevh1 = max(M["h1"][0], M["h1"][1])
            prevl2 = M["fstop"][3]

            W1CH = ("LWa", "LWb", "LWc", "LWd")
            W2CH = ("L2a", "L2b", "L2c", "L2d")

            def ld_w1(F, gate):
                return p.op("sync", lambda sy, F=F: sy.dma_start(
                    out=w1s[:, 1024 * (F % 4):1024 * (F % 4) + 1024], in_=w1t_d[F]),
                    wait=gate, inc=(W1CH[F % 4], 16))

            def ld_w2(F, half, gate):
                return p.op("gpsimd", lambda g, F=F, half=half: g.dma_start(
                    out=w2s[:, 512 * (F % 4):512 * (F % 4) + 512],
                    in_=w2t_d[F][:, 512 * half:512 * half + 512]),
                    wait=gate, inc=(W2CH[F % 4], 16))

            for FF in range(4):
                ld_w1(FF, [("PE", prevh1)] if prevh1 else [])
                ld_w2(FF, 0, [("PE", prevl2)] if prevl2 else [])
            for F in range(33):
                if F == 16 and ss + 1 < NSS:
                    ln_trans(0, xT)
                    ln_trans(1, xT)
                    phaseA_part(ss + 1, 1)
                if F < 32:
                    par = F % 2
                    slot = F % 4
                    bank = pA if par == 0 else pB
                    w = [(W1CH[slot], p.cnt[W1CH[slot]]),
                         ("AC", max(M["ac_yT"], M["gelu"][par]))]
                    if F == 0:
                        w.append(("DV", dv_y1b))
                    pes = 0
                    for dc in range(8):
                        pes = p.op("tensor", lambda t, dc=dc, slot=slot, bank=bank: t.matmul(
                            bank[:, :], w1s[:, 1024 * slot + 128 * dc:1024 * slot + 128 * dc + 128],
                            yT[:, 512 * dc:512 * dc + 512], start=(dc == 0), stop=(dc == 7)),
                            wait=(w if dc == 0 else ()),
                            inc=(("PE", 1) if dc == 7 else None))
                    M["h1"][par] = pes
                    M["gelu"][par] = p.op("scalar", lambda sc, F=F, bank=bank: sc.activation(
                        out=HH[:, 512 * F:512 * F + 512], in_=bank[:, :], func=AF.Gelu,
                        scale=0.5, bias=b1c[:, F:F + 1]),
                        wait=[("PE", pes)], inc=("AC", 1))
                if F >= 1:
                    Fp = F - 1
                    parp = Fp % 2
                    slotp = Fp % 4
                    for tc in range(4):
                        w = [("AC", M["gelu"][parp])]
                        if tc == 0:
                            w.append((W2CH[slotp], p.cnt[W2CH[slotp]]))
                        if Fp == 0:
                            w.append(("DV", dv_y1b))
                            if ss > 0:
                                w.append(("DV", M["epi1"][tc]))
                        pes = p.op("tensor", lambda t, Fp=Fp, tc=tc, slotp=slotp, bk=banks[tc]: t.matmul(
                            bk[:, :], HH[:, 512 * Fp + 128 * tc:512 * Fp + 128 * tc + 128],
                            w2s[:, 512 * slotp:512 * slotp + 512],
                            start=(Fp == 0), stop=(Fp == 31)),
                            wait=(w if (tc == 0 or Fp == 0) else ()), inc=("PE", 1))
                        if tc == 3:
                            M["ffn_o3"] = pes
                        if Fp == 31:
                            M["fstop"][tc] = pes
                    if Fp + 4 < 32:
                        ld_w2(Fp + 4, 0, [("PE", M["ffn_o3"])])
                if F < 32 and F + 4 < 32:
                    ld_w1(F + 4, [("PE", M["h1"][F % 2])])
            for tc in range(4):
                w = [("PE", M["fstop"][tc])]
                dve = p.op("vector", lambda v, tc=tc, bk=banks[tc]: v.scalar_tensor_tensor(
                    out=y1[:, 1024 * tc:1024 * tc + 512], in0=bk[:, :], scalar=0.5,
                    in1=y1[:, 1024 * tc:1024 * tc + 512], op0=ALU.mult, op1=ALU.add),
                    wait=w, inc=("DV", 1))
                M["epi0"][tc] = dve
                p.op("gpsimd", lambda g, ss=ss, tc=tc: g.dma_start(
                    out=y_d[512 * ss + 128 * tc:512 * ss + 128 * tc + 128, 0:512],
                    in_=y1[:, 1024 * tc:1024 * tc + 512]),
                    wait=[("DV", dve)], inc=("ST", 16))
            prevcar = M["fstop"][3]
            for FF in range(4):
                ld_w2(FF, 1, [("PE", prevcar)])
            for F in range(32):
                slot = F % 4
                for tc in range(4):
                    w = []
                    if tc == 0:
                        w.append((W2CH[slot], p.cnt[W2CH[slot]]))
                    if F == 0:
                        w.append(("DV", M["epi0"][tc]))
                    pes = p.op("tensor", lambda t, F=F, tc=tc, slot=slot, bk=banks[tc]: t.matmul(
                        bk[:, :], HH[:, 512 * F + 128 * tc:512 * F + 128 * tc + 128],
                        w2s[:, 512 * slot:512 * slot + 512],
                        start=(F == 0), stop=(F == 31)),
                        wait=(w if (tc == 0 or F == 0) else ()), inc=("PE", 1))
                    if tc == 3:
                        M["ffn_o3"] = pes
                    if F == 31:
                        M["fstop"][tc] = pes
                if F + 4 < 32:
                    ld_w2(F + 4, 1, [("PE", M["ffn_o3"])])
            for tc in range(4):
                w = [("PE", M["fstop"][tc])]
                dve = p.op("vector", lambda v, tc=tc, bk=banks[tc]: v.scalar_tensor_tensor(
                    out=y1[:, 1024 * tc + 512:1024 * tc + 1024], in0=bk[:, :], scalar=0.5,
                    in1=y1[:, 1024 * tc + 512:1024 * tc + 1024], op0=ALU.mult, op1=ALU.add),
                    wait=w, inc=("DV", 1))
                M["epi1"][tc] = dve
                p.op("gpsimd", lambda g, ss=ss, tc=tc: g.dma_start(
                    out=y_d[512 * ss + 128 * tc:512 * ss + 128 * tc + 128, 512:1024],
                    in_=y1[:, 1024 * tc + 512:1024 * tc + 1024]),
                    wait=[("DV", dve)], inc=("ST", 16))
            if ss + 1 < NSS:
                ln_trans(2, xT)
                ln_trans(3, xT)
                M["ac_xT"] = p.cnt["AC"]

        p.wait("gpsimd", "ST", p.cnt["ST"])
        p.emit(block, sems)
    return nc


_NC_CACHE = [None]


def kernel(L0_attn_x, L0_attn_Wq, L0_attn_Wk, L0_attn_Wv, L0_attn_Wo, L0_attn_betas,
           L0_ffn_W1, L0_ffn_b1, L0_ffn_W2, L0_ffn_b2):
    x = np.ascontiguousarray(np.asarray(L0_attn_x, dtype=np.float32))
    wq = np.ascontiguousarray(np.asarray(L0_attn_Wq, dtype=np.float32))
    wk = np.ascontiguousarray(np.asarray(L0_attn_Wk, dtype=np.float32))
    wv = np.ascontiguousarray(np.asarray(L0_attn_Wv, dtype=np.float32))
    wo = np.ascontiguousarray(np.asarray(L0_attn_Wo, dtype=np.float32))
    bet = np.ascontiguousarray(np.asarray(L0_attn_betas, dtype=np.float32).reshape(16, 64))
    w1 = np.ascontiguousarray(np.asarray(L0_ffn_W1, dtype=np.float32))
    b1 = np.ascontiguousarray(np.asarray(L0_ffn_b1, dtype=np.float32))
    w2 = np.ascontiguousarray(np.asarray(L0_ffn_W2, dtype=np.float32))
    b2 = np.ascontiguousarray(np.asarray(L0_ffn_b2, dtype=np.float32))

    if _NC_CACHE[0] is None:
        _NC_CACHE[0] = build_program()
    nc = _NC_CACHE[0]

    in_maps = []
    for core in range(8):
        b, q = divmod(core, 4)
        xc = np.ascontiguousarray(
            x[b].reshape(32, 4, 128, 1024)[:, q].reshape(4096, 1024))
        in_maps.append({
            "x": xc, "wq": wq, "wk": wk, "wv": wv, "wo": wo,
            "betas": np.ascontiguousarray(bet[4 * q:4 * q + 4]),
            "w1": w1, "b1": b1, "w2": w2, "b2": b2,
        })
    res = run_bass_kernel_spmd(nc, in_maps, core_ids=list(range(8)))
    out = np.empty((2, 32, 4, 128, 1024), dtype=np.float32)
    for core in range(8):
        b, q = divmod(core, 4)
        out[b, :, q] = res.results[core]["y"].reshape(32, 128, 1024)
    return out.reshape(2, 16384, 1024)


# version 38
# speedup vs baseline: 1.3128x; 1.3128x over previous
import numpy as np
import ml_dtypes
import concourse.bass as bass
import concourse.mybir as mybir
from concourse.bass_utils import run_bass_kernel_spmd
from contextlib import ExitStack

F32 = mybir.dt.float32
F32R = mybir.dt.float32r
BF16 = mybir.dt.bfloat16
AF = mybir.ActivationFunctionType
ALU = mybir.AluOpType
AX = mybir.AxisListType

NSS = 8
ENGS = ("sync", "gpsimd", "vector", "scalar", "tensor")
SEMS = ("LD", "LPW", "LX", "LV", "LWa", "LWb", "LWc", "LWd", "L2a", "L2b",
        "L2c", "L2d", "SA", "SB", "SC", "SSY", "SSC", "ST", "PE", "DV", "AC",
        "PL", "PD")


class Prog:

    def __init__(self):
        self.q = {e: [] for e in ENGS}
        self.cnt = {s: 0 for s in SEMS}

    def op(self, eng, fn, inc=None, wait=()):
        w = tuple((s, v) for (s, v) in wait if v > 0)
        self.q[eng].append(("op", fn, w, inc))
        if inc is not None:
            self.cnt[inc[0]] += inc[1]
            return self.cnt[inc[0]]
        return None

    def drain(self, eng):
        self.q[eng].append(("drain", None, (), None))

    def wait(self, eng, sem, val):
        if val > 0:
            self.q[eng].append(("wait", None, ((sem, val),), None))

    def emit(self, block, sems):
        def mk(entries):
            def _(e):
                for kind, fn, waits, inc in entries:
                    for s, v in waits:
                        e.wait_ge(sems[s], v)
                    if kind == "drain":
                        e.drain()
                    elif kind == "op":
                        ins = fn(e)
                        if inc is not None:
                            ins.then_inc(sems[inc[0]], inc[1])
            return _
        block.sync(mk(self.q["sync"]))
        block.gpsimd(mk(self.q["gpsimd"]))
        block.vector(mk(self.q["vector"]))
        block.scalar(mk(self.q["scalar"]))
        block.tensor(mk(self.q["tensor"]))


S2ALL = frozenset(["b2", "qvtr", "evac", "mem", "attd", "membuf", "num",
                   "rr", "bc", "u", "fin"])


def build_program(stop="F", c_lim=16, s2p=S2ALL):
    _ph = {"prep": 0, "A": 1, "B": 2, "C": 3, "D": 4, "E": 5, "F": 6}[stop]
    nc = bass.Bass("TRN2", target_bir_lowering=False)
    x_d = nc.dram_tensor("x", [4096, 1024], F32, kind="ExternalInput").ap()
    wq_d = nc.dram_tensor("wq", [1024, 1024], F32, kind="ExternalInput").ap()
    wk_d = nc.dram_tensor("wk", [1024, 1024], F32, kind="ExternalInput").ap()
    wv_d = nc.dram_tensor("wv", [1024, 1024], F32, kind="ExternalInput").ap()
    wo_d = nc.dram_tensor("wo", [1024, 1024], F32, kind="ExternalInput").ap()
    bet_d = nc.dram_tensor("betas", [4, 64], F32, kind="ExternalInput").ap()
    w1_d = nc.dram_tensor("w1", [4096, 1024], F32, kind="ExternalInput").ap()
    b1_d = nc.dram_tensor("b1", [4096], F32, kind="ExternalInput").ap()
    w2_d = nc.dram_tensor("w2", [1024, 4096], F32, kind="ExternalInput").ap()
    b2_d = nc.dram_tensor("b2", [1024], F32, kind="ExternalInput").ap()
    y_d = nc.dram_tensor("y", [4096, 1024], F32, kind="ExternalOutput").ap()
    w1t_d = nc.dram_tensor("w1t_s", [32, 128, 1024], BF16, kind="Internal").ap()
    w2t_d = nc.dram_tensor("w2t_s", [32, 128, 1024], BF16, kind="Internal").ap()
    wvo_d = nc.dram_tensor("wvo_s", [2, 128, 8192], BF16, kind="Internal").ap()

    p = Prog()

    with ExitStack() as ctx:
        E = ctx.enter_context
        ident = E(nc.sbuf_tensor("ident", [128, 128], BF16))
        identf = E(nc.sbuf_tensor("identf", [64, 64], F32))
        onesf = E(nc.sbuf_tensor("onesf", [1, 128], F32))
        onesc = E(nc.sbuf_tensor("onesc", [128, 1], F32))
        wq_sb = E(nc.sbuf_tensor("wq_sb", [128, 8192], BF16))
        wk_sb = E(nc.sbuf_tensor("wk_sb", [128, 8192], BF16))
        wvo_sb = E(nc.sbuf_tensor("wvo_sb", [128, 8192], BF16))
        b1c = E(nc.sbuf_tensor("b1c", [128, 32], F32))
        b2bc = E(nc.sbuf_tensor("b2bc", [128, 1024], F32))
        g4 = E(nc.sbuf_tensor("g4", [64, 4], F32))
        omg4 = E(nc.sbuf_tensor("omg4", [64, 4], F32))
        gz4 = E(nc.sbuf_tensor("gz4", [64, 8], F32))

        zc_pref = E(nc.sbuf_tensor("zc_pref", [64, 4], F32))

        sems = {}
        for s in SEMS:
            sems[s] = E(nc.semaphore(s))
        pA = E(nc.psum_tensor("pA", [128, 512], F32))
        pB = E(nc.psum_tensor("pB", [128, 512], F32))
        pC = E(nc.psum_tensor("pC", [128, 512], F32))
        pD = E(nc.psum_tensor("pD", [128, 512], F32))
        pS0 = E(nc.psum_tensor("pS0", [128, 512], F32))
        pS1 = E(nc.psum_tensor("pS1", [128, 512], F32))
        pAT = E(nc.psum_tensor("pAT", [128, 512], F32))
        pTR = E(nc.psum_tensor("pTR", [128, 1024], BF16))
        block = E(nc.Block())

        with ExitStack() as pctx:
            PP = pctx.enter_context
            wstage = PP(nc.sbuf_tensor("wstage", [128, 9216], F32))
            sgn1 = PP(nc.sbuf_tensor("sgn1", [128, 2048], F32))
            sgnm = PP(nc.sbuf_tensor("sgnm", [128, 2048], F32))
            wtc = PP(nc.sbuf_tensor("wtc", [128, 2048], BF16))
            wasm = PP(nc.sbuf_tensor("wasm", [128, 8192], BF16))
            wabs = PP(nc.sbuf_tensor("wabs", [128, 32], F32))
            thrc = PP(nc.sbuf_tensor("thrc", [128, 2], F32))
            trow = PP(nc.sbuf_tensor("trow", [1, 8], F32))
            b1row = PP(nc.sbuf_tensor("b1row", [1, 4096], F32))
            graw = PP(nc.sbuf_tensor("graw", [64, 4], F32))

            p.op("gpsimd", lambda g: g.iota(ident[:, :], [[1, 128]], channel_multiplier=-1,
                                            allow_small_or_imprecise_dtypes=True))
            p.drain("gpsimd")
            p.op("gpsimd", lambda g: g.tensor_scalar(out=ident[:, :], in0=ident[:, :],
                                                     scalar1=0.0, scalar2=None, op0=ALU.is_equal))
            p.op("gpsimd", lambda g: g.iota(identf[:, :], [[1, 64]], channel_multiplier=-1,
                                            allow_small_or_imprecise_dtypes=True))
            p.drain("gpsimd")
            p.op("gpsimd", lambda g: g.tensor_scalar(out=identf[:, :], in0=identf[:, :],
                                                     scalar1=0.0, scalar2=None, op0=ALU.is_equal))
            p.op("gpsimd", lambda g: g.memset(onesf[:, :], 1.0), inc=("PL", 1))
            p.op("gpsimd", lambda g: g.memset(onesc[:, :], 1.0))
            p.op("gpsimd", lambda g: g.memset(zc_pref[:, :], 0.0), inc=("PL", 1))
            pl_init = p.cnt["PL"]

            p.op("sync", lambda s: s.dma_start(out=graw[:, :], in_=bet_d.rearrange("h j -> j h"),
                                               allow_slow_non_contiguous=True), inc=("LD", 16))
            p.op("sync", lambda s: s.dma_start(out=b1row[0:1, :],
                                               in_=b1_d.rearrange("(a b) -> a b", a=1)), inc=("LD", 16))
            ld0 = p.op("sync", lambda s: s.dma_start(out=b2bc[0:1, :],
                                                     in_=b2_d.rearrange("(a b) -> a b", a=1)), inc=("LD", 16))
            ac0 = p.op("scalar", lambda s: s.activation(out=g4[:, :], in_=graw[:, :], func=AF.Sigmoid),
                       wait=[("LD", ld0)], inc=("AC", 1))
            p.op("vector", lambda v: v.tensor_scalar(out=omg4[:, :], in0=g4[:, :], scalar1=-1.0,
                                                     scalar2=1.0, op0=ALU.mult, op1=ALU.add),
                 wait=[("AC", ac0)], inc=("DV", 1))

            peb = 0
            for F in range(32):
                peb = p.op("tensor", lambda t, F=F: t.transpose(
                    pA[:, F:F + 1], b1row[0:1, 128 * F:128 * F + 128], onesf[0:1, 0:1]),
                    wait=([("LD", ld0), ("PL", pl_init)] if F == 0 else ()), inc=("PE", 1))
            acb1 = p.op("scalar", lambda s: s.copy(out=b1c[:, :], in_=pA[:, 0:32]),
                        wait=[("PE", peb)], inc=("AC", 1))
            for h in range(2):
                peb = p.op("tensor", lambda t, h=h: t.matmul(
                    pB[:, 0:512], onesf[0:1, 0:128], b2bc[0:1, 512 * h:512 * h + 512],
                    start=True, stop=True),
                    wait=[("AC", acb1 if h == 0 else p.cnt["AC"])], inc=("PE", 1))
                p.op("scalar", lambda s, h=h: s.copy(out=b2bc[:, 512 * h:512 * h + 512],
                                                     in_=pB[:, 0:512]),
                     wait=[("PE", peb)], inc=("AC", 1))

            QENG = ("sync", "scalar", "gpsimd")
            SCH = ("LPW", "SA", "L2a", "LX", "SB", "L2b", "LV", "SC", "L2c")
            NSLOT = 9
            slot_gate = [[] for _ in range(NSLOT)]
            stq = [0]

            def emit_load(slot, src, gates):
                ch = SCH[slot]
                p.op(QENG[slot % 3], lambda e, src=src, slot=slot: e.dma_start(
                    out=wstage[:, 1024 * slot:1024 * slot + 1024], in_=src),
                    wait=gates, inc=(ch, 16))

            def emit_store(dst, src, waits):
                sch = ("SSY", "SSC")[stq[0] % 2]
                eng = ("sync", "scalar")[stq[0] % 2]
                stq[0] += 1
                p.op(eng, lambda e, dst=dst, src=src: e.dma_start(out=dst, in_=src),
                     wait=waits, inc=(sch, 16))

            def run_pipe(jobs):
                n = len(jobs)
                for i in range(min(NSLOT, n)):
                    jobs[i][0](i % NSLOT, slot_gate[i % NSLOT])
                for i in range(n):
                    g = jobs[i][1](i % NSLOT)
                    slot_gate[i % NSLOT] = g
                    if i + NSLOT < n:
                        jobs[i + NSLOT][0](i % NSLOT, g)

            def mk_cast(wd, mi, dc):
                def load(slot, gates):
                    emit_load(slot, wd[128 * dc:128 * dc + 128, :], gates)

                def consume(slot):
                    ch = SCH[slot]
                    if mi < 2:
                        wsb = wq_sb if mi == 0 else wk_sb
                        v = p.op("vector", lambda ve, wsb=wsb, dc=dc, slot=slot: ve.tensor_copy(
                            out=wsb[:, 1024 * dc:1024 * dc + 1024],
                            in_=wstage[:, 1024 * slot:1024 * slot + 1024]),
                            wait=[(ch, p.cnt[ch])], inc=("DV", 1))
                        return [("DV", v)]
                    w = [(ch, p.cnt[ch])]
                    if mi == 3 and dc == 0:
                        w += [("SSY", p.cnt["SSY"]), ("SSC", p.cnt["SSC"])]
                    v = p.op("vector", lambda ve, dc=dc, slot=slot: ve.tensor_copy(
                        out=wvo_sb[:, 1024 * dc:1024 * dc + 1024],
                        in_=wstage[:, 1024 * slot:1024 * slot + 1024]),
                        wait=w, inc=("DV", 1))
                    emit_store(wvo_d[mi - 2][:, 1024 * dc:1024 * dc + 1024],
                               wvo_sb[:, 1024 * dc:1024 * dc + 1024], [("DV", v)])
                    return [("DV", v)]
                return (load, consume)

            jobs = []
            for mi, wd in enumerate([wq_d, wk_d, wv_d, wo_d]):
                for dc in range(8):
                    jobs.append(mk_cast(wd, mi, dc))
            run_pipe(jobs)

            for wi in range(2):
                if wi == 0:
                    def srcw(c):
                        return w1_d[128 * c:128 * c + 128, :]
                else:
                    def srcw(c):
                        return w2_d[128 * (c % 8):128 * (c % 8) + 128,
                                    1024 * (c // 8):1024 * (c // 8) + 1024]

                def mk_stat(c, srcw=srcw):
                    def load(slot, gates):
                        emit_load(slot, srcw(c), gates)

                    def consume(slot):
                        ch = SCH[slot]
                        v = p.op("vector", lambda e, c=c, slot=slot: e.tensor_reduce(
                            out=wabs[:, c:c + 1],
                            in_=wstage[:, 1024 * slot:1024 * slot + 1024],
                            axis=AX.X, op=ALU.add, apply_absolute_value=True),
                            wait=[(ch, p.cnt[ch])], inc=("DV", 1))
                        return [("DV", v)]
                    return (load, consume)

                run_pipe([mk_stat(c) for c in range(32)])
                p.drain("vector")
                dvs = p.op("vector", lambda v: v.tensor_reduce(
                    out=wabs[:, 0:1], in_=wabs[:, 0:32], axis=AX.X, op=ALU.add),
                    wait=[("PL", p.cnt["PL"])], inc=("DV", 1))
                pet = p.op("tensor", lambda t: t.matmul(pA[0:1, 0:1], wabs[:, 0:1], onesc[:, 0:1],
                                                        start=True, stop=True),
                           wait=[("DV", dvs), ("AC", p.cnt["AC"])], inc=("PE", 1))
                p.op("vector", lambda v: v.tensor_scalar(
                    out=trow[0:1, 0:1], in0=pA[0:1, 0:1], scalar1=1.0 / (4096.0 * 1024.0),
                    scalar2=1e-8, op0=ALU.mult, op1=ALU.max), wait=[("PE", pet)])
                p.drain("vector")
                p.op("vector", lambda v: v.tensor_scalar(
                    out=trow[0:1, 1:2], in0=trow[0:1, 0:1], scalar1=0.5, scalar2=None, op0=ALU.mult))
                p.drain("vector")
                dvs = p.op("vector", lambda v: v.tensor_scalar(
                    out=trow[0:1, 2:3], in0=trow[0:1, 1:2], scalar1=-1.0, scalar2=None,
                    op0=ALU.mult), inc=("DV", 1))
                pet = p.op("tensor", lambda t: t.matmul(pA[:, 0:2], onesf[0:1, 0:128],
                                                        trow[0:1, 1:3], start=True, stop=True),
                           wait=[("DV", dvs), ("AC", p.cnt["AC"])], inc=("PE", 1))
                acthr = p.op("scalar", lambda s: s.copy(out=thrc[:, :], in_=pA[:, 0:2]),
                             wait=[("PE", pet)], inc=("AC", 1))

                st8 = {"stt": [0, 0], "tr": [0, 0], "evac": 0}

                def mk_xf(c, srcw=srcw, wi=wi, acthr=acthr, st8=st8):
                    def load(slot, gates):
                        emit_load(slot, srcw(c), gates)

                    def consume(slot):
                        ch = SCH[slot]
                        par = c % 2
                        w0 = [(ch, p.cnt[ch])]
                        if c == 0:
                            w0 = w0 + [("AC", acthr)]
                        wp = [("DV", st8["stt"][par])] if st8["stt"][par] else []
                        a1 = p.op("scalar", lambda s, par=par, slot=slot: s.activation(
                            out=sgn1[:, 1024 * par:1024 * par + 1024],
                            in_=wstage[:, 1024 * slot:1024 * slot + 1024],
                            func=AF.Sign, bias=thrc[:, 1:2]), wait=w0 + wp, inc=("AC", 1))
                        dvm = p.op("vector", lambda v, par=par, slot=slot: v.tensor_scalar(
                            out=sgnm[:, 1024 * par:1024 * par + 1024],
                            in0=wstage[:, 1024 * slot:1024 * slot + 1024],
                            scalar1=thrc[:, 1:2], scalar2=2.0,
                            op0=ALU.is_lt, op1=ALU.mult), wait=w0 + wp, inc=("DV", 1))
                        p.drain("vector")
                        wt = [("AC", a1)]
                        if st8["tr"][par]:
                            wt.append(("PE", st8["tr"][par]))
                        st8["stt"][par] = p.op("vector", lambda v, par=par: v.scalar_tensor_tensor(
                            out=wtc[:, 1024 * par:1024 * par + 1024],
                            in0=sgn1[:, 1024 * par:1024 * par + 1024], scalar=1.0,
                            in1=sgnm[:, 1024 * par:1024 * par + 1024],
                            op0=ALU.add, op1=ALU.subtract), wait=wt, inc=("DV", 1))
                        pel = 0
                        for k in range(8):
                            w = ()
                            if k == 0:
                                w = [("DV", st8["stt"][par]), ("AC", st8["evac"])]
                            pel = p.op("tensor", lambda t, k=k, par=par: t.transpose(
                                pTR[:, 128 * k:128 * k + 128],
                                wtc[:, 1024 * par + 128 * k:1024 * par + 128 * k + 128],
                                ident[:, :]), wait=w, inc=("PE", 1))
                        st8["tr"][par] = pel
                        if wi == 0:
                            sch = ("SSY", "SSC")[par]
                            acw = p.op("scalar", lambda s, par=par: s.copy(
                                out=wasm[:, 1024 * par:1024 * par + 1024], in_=pTR[:, 0:1024]),
                                wait=[("PE", pel), (sch, p.cnt[sch])], inc=("AC", 1))
                            st8["evac"] = acw
                            p.op(("sync", "scalar")[par], lambda e, c=c, par=par: e.dma_start(
                                out=w1t_d[c], in_=wasm[:, 1024 * par:1024 * par + 1024]),
                                wait=[("AC", acw)], inc=(sch, 16))
                        else:
                            j, Dc = divmod(c, 8)
                            wv_ = [("PE", pel)]
                            if Dc == 0:
                                wv_ += [("SSY", p.cnt["SSY"]), ("SSC", p.cnt["SSC"])]
                            dst = wasm[:, :].rearrange("p (k c) -> p k c", k=8)[:, :, 128 * Dc:128 * Dc + 128]
                            acw = p.op("scalar", lambda s, dst=dst: s.copy(
                                out=dst, in_=pTR[:, :].rearrange("p (k c) -> p k c", k=8)),
                                wait=wv_, inc=("AC", 1))
                            st8["evac"] = acw
                            if Dc == 7:
                                for k in range(8):
                                    emit_store(w2t_d[8 * j + k],
                                               wasm[:, 1024 * k:1024 * k + 1024],
                                               [("AC", acw)])
                        return [("AC", a1), ("DV", dvm)]
                    return (load, consume)

                run_pipe([mk_xf(c) for c in range(32)])

            p.wait("gpsimd", "SSY", p.cnt["SSY"])
            p.wait("gpsimd", "SSC", p.cnt["SSC"])
            for eng in ("gpsimd", "vector", "scalar", "tensor"):
                p.drain(eng)
                p.op(eng, lambda e: e.nop(), inc=("PD", 1))
            p.op("sync", lambda e: e.nop(), inc=("PD", 1))
            for eng in ENGS:
                p.wait(eng, "PD", 5)

        T1 = E(nc.sbuf_tensor("T1", [128, 8192], BF16))
        T2 = E(nc.sbuf_tensor("T2", [128, 8192], BF16))
        xT = E(nc.sbuf_tensor("xT", [128, 4096], BF16))
        yT = E(nc.sbuf_tensor("yT", [128, 4096], BF16))
        xs = E(nc.sbuf_tensor("xs", [128, 2048], F32))
        lnp = E(nc.sbuf_tensor("lnp", [128, 2048], BF16))
        stats = E(nc.sbuf_tensor("stats", [128, 16], F32))
        y1 = E(nc.sbuf_tensor("y1", [128, 4096], F32))
        Pt = E(nc.sbuf_tensor("Pt", [128, 4096], BF16))
        vaug = E(nc.sbuf_tensor("vaug", [128, 520], BF16))
        sqtm = E(nc.sbuf_tensor("sqtm", [128, 512], BF16))

        zaccs = E(nc.sbuf_tensor("zaccs", [64, 8], F32))
        znt = E(nc.sbuf_tensor("znt", [64, 8], F32))
        zrec = E(nc.sbuf_tensor("zrec", [64, 16], F32))
        membuf = E(nc.sbuf_tensor("membuf", [64, 130], BF16))
        ATT = E(nc.sbuf_tensor("ATT", [128, 4096], BF16))
        HH = E(nc.sbuf_tensor("HH", [128, 16384], BF16))
        w1s = E(nc.sbuf_tensor("w1s", [128, 4096], BF16))
        w2s = E(nc.sbuf_tensor("w2s", [128, 2048], BF16))
        outst = E(nc.sbuf_tensor("outst", [64, 2048], F32))
        ssq = lnp
        SQV = HH[:, 0:8192]
        SQ = HH[0:64, 0:8192]
        zsc = xs[0:64, 0:2048]
        mem_pref = T1[64:128, 4232:4744].bitcast(F32)
        rr_in = T1[64:65, 0:4096].bitcast(F32)
        numsb = lnp[0:64, 0:2048].bitcast(F32)
        rrow = E(nc.sbuf_tensor("rrow", [1, 1024], F32R))
        gblkZ = E(nc.sbuf_tensor("gblkZ", [1, 1024], F32R))
        bcB = y1[:, 3072:4096]
        uB = outst[0:64, 0:1024]
        uA = outst[0:64, 1024:2048]

        p.op("gpsimd", lambda g: g.memset(
            vaug[:, :].rearrange("p (k c) -> p k c", c=65)[:, :, 64:65], 1.0), inc=("PL", 1))
        p.op("gpsimd", lambda g: g.memset(mem_pref[:, :], 0.0), inc=("PL", 1))
        p.op("gpsimd", lambda g: g.memset(
            membuf[:, :].rearrange("p (k c) -> p k c", c=65)[:, :, 64:65], 1.0), inc=("PL", 1))
        p.op("gpsimd", lambda g: g.memset(rr_in[0:1, 0:1024], 0.0), inc=("PL", 1))
        pl_ms = p.cnt["PL"]
        dv_z = p.op("vector", lambda v: v.tensor_copy(out=gblkZ[:, :],
                                                      in_=rr_in[0:1, 0:1024]),
                    wait=[("PL", pl_ms)], inc=("DV", 1))
        pe_gb = 0
        for k in range(8):
            src = omg4[:, k:k + 1] if k < 4 else g4[:, k - 4:k - 4 + 1]
            pe_gb = p.op("tensor", lambda t, k=k, src=src: t.transpose(
                pC[0:1, 64 * k:64 * k + 64], src, identf[0:64, 0:64]),
                wait=([("DV", p.cnt["DV"]), ("AC", p.cnt["AC"])] if k == 0 else ()),
                inc=("PE", 1))
        ac_gb = p.op("scalar", lambda s: s.copy(out=rr_in[0:1, 0:512], in_=pC[0:1, 0:512]),
                     wait=[("PE", pe_gb), ("DV", dv_z)], inc=("AC", 1))
        dv_gb = 0
        for ho in range(4):
            p.op("vector", lambda v, ho=ho: v.tensor_copy(
                out=gblkZ[0:1, 128 * ho:128 * ho + 64],
                in_=rr_in[0:1, 256 + 64 * ho:256 + 64 * ho + 64]),
                wait=([("AC", ac_gb), ("PL", pl_ms)] if ho == 0 else ()), inc=("DV", 1))
            dv_gb = p.op("vector", lambda v, ho=ho: v.tensor_copy(
                out=gblkZ[0:1, 512 + 128 * ho + 64:512 + 128 * ho + 128],
                in_=rr_in[0:1, 64 * ho:64 * ho + 64]), inc=("DV", 1))
        p.wait("tensor", "PL", p.cnt["PL"])

        QT = T1[0:64, :]
        VT = SQV[64:128, :]
        KT = T2[0:64, :]
        QTr = T1[0:64, :].rearrange("i (hs r m) -> i hs r m", r=16, m=32)
        VTr = SQV[64:128, :].rearrange("i (hs r m) -> i hs r m", r=16, m=32)
        KTr = T2[0:64, :].rearrange("i (hs r m) -> i hs r m", r=16, m=32)

        M = {
            "vev": 0, "sqev": 0, "memadd": 0, "memlast": 0, "numpe": 0,
            "bcpe": 0, "bcbe": 0, "u1_0": 0, "u1_1": 0, "u2": 0, "rrB": 0,
            "lnr_0": 0, "lnr_1": 0, "numev": 0,
            "expR": 0, "ek_last": 0, "sq_s": [0, 0, 0, 0], "gz_s": [0, 0, 0, 0],
            "atr_evac": 0, "ln_tr": [0, 0, 0, 0], "peB0": 0,
            "ln_ap": [0, 0, 0, 0], "ln_sq": [0, 0, 0, 0], "ln_red": [0, 0, 0, 0],
            "resid": [0, 0, 0, 0], "expq": [0, 0, 0],
            "h1": [0, 0], "gelu": [0, 0],
            "ffn_o3": 0, "epi0": [0, 0, 0, 0], "epi1": [0, 0, 0, 0],
            "fstop": [0, 0, 0, 0], "wo_last": 0,
            "ac_xT": 0, "ac_yT": 0,
        }

        def newton(dst, srcv, scr):
            p.drain("vector")
            p.op("vector", lambda v: v.reciprocal(out=dst, in_=srcv))
            p.drain("vector")
            p.op("vector", lambda v: v.tensor_mul(out=scr, in0=srcv, in1=dst))
            p.drain("vector")
            p.op("vector", lambda v: v.tensor_scalar(out=scr, in0=scr, scalar1=2.0,
                                                     scalar2=-1.0, op0=ALU.subtract,
                                                     op1=ALU.mult))
            p.drain("vector")
            p.op("vector", lambda v: v.tensor_mul(out=dst, in0=dst, in1=scr),
                 inc=("DV", 1))
            p.drain("vector")

        def ln_stats(s, src_ap, src_wait):
            par = s % 2
            dvr = p.op("vector", lambda v: v.tensor_reduce(
                out=stats[:, s:s + 1], in_=src_ap, axis=AX.X, op=ALU.add),
                wait=src_wait, inc=("DV", 1))
            M["ln_red"][s] = dvr
            p.drain("scalar")
            M["ln_sq"][s] = p.op("scalar", lambda sc, par=par: sc.activation(
                out=ssq[:, 1024 * par:1024 * par + 1024], in_=src_ap, func=AF.Square,
                accum_out=stats[:, 4 + s:5 + s]),
                wait=list(src_wait) + [("PE", M["ln_tr"][(s + 2) % 4]),
                                       ("DV", M["u2"])], inc=("AC", 1))
            p.drain("vector")
            p.op("vector", lambda v: v.tensor_scalar(
                out=stats[:, s:s + 1], in0=stats[:, s:s + 1], scalar1=1.0 / 1024.0,
                scalar2=None, op0=ALU.mult))
            p.op("vector", lambda v: v.tensor_scalar(
                out=stats[:, 4 + s:5 + s], in0=stats[:, 4 + s:5 + s], scalar1=1.0 / 1024.0,
                scalar2=None, op0=ALU.mult), wait=[("AC", M["ln_sq"][s])])
            p.drain("vector")
            p.op("vector", lambda v: v.tensor_mul(
                out=stats[:, 12 + s:13 + s], in0=stats[:, s:s + 1], in1=stats[:, s:s + 1]))
            p.drain("vector")
            dvv = p.op("vector", lambda v: v.tensor_scalar(
                out=stats[:, 4 + s:5 + s], in0=stats[:, 4 + s:5 + s],
                scalar1=stats[:, 12 + s:13 + s], scalar2=1e-5,
                op0=ALU.subtract, op1=ALU.add), inc=("DV", 1))
            p.op("scalar", lambda sc: sc.activation(
                out=stats[:, 4 + s:5 + s], in_=stats[:, 4 + s:5 + s], func=AF.Ln),
                wait=[("DV", dvv)])
            p.drain("scalar")
            acs = p.op("scalar", lambda sc: sc.activation(
                out=stats[:, 8 + s:9 + s], in_=stats[:, 4 + s:5 + s],
                func=AF.Exp, scale=-0.5), inc=("AC", 1))
            aw = [("PE", M["ln_tr"][(s + 2) % 4]), ("AC", acs)]
            M["ln_ap"][s] = p.op("vector", lambda v: v.tensor_scalar(
                out=lnp[:, 1024 * par:1024 * par + 1024], in0=src_ap,
                scalar1=stats[:, s:s + 1], scalar2=stats[:, 8 + s:9 + s],
                op0=ALU.subtract, op1=ALU.mult), wait=aw, inc=("DV", 1))

        def ln_trans(s, dstT):
            par = s % 2
            pel = 0
            for dc in range(8):
                w = ()
                if dc == 0:
                    w = [("DV", max(M["ln_ap"][s], M["vev"], M["sqev"])),
                         ("AC", M["atr_evac"])]
                pel = p.op("tensor", lambda t, dc=dc, par=par: t.transpose(
                    pTR[:, 128 * dc:128 * dc + 128],
                    lnp[:, 1024 * par + 128 * dc:1024 * par + 128 * dc + 128],
                    ident[:, :]), wait=w, inc=("PE", 1))
            M["ln_tr"][s] = pel
            dstv = dstT[:, :].rearrange("p (dc c) -> p dc c", c=512)[:, :, 128 * s:128 * s + 128]
            M["atr_evac"] = p.op("scalar", lambda sc, dstv=dstv: sc.copy(
                out=dstv, in_=pTR[:, :].rearrange("p (dc c) -> p dc c", dc=8)),
                wait=[("PE", pel)], inc=("AC", 1))

        def ldx_op(ss, s, w):
            return p.op("sync", lambda sy, ss=ss, s=s: sy.dma_start(
                out=xs[:, 1024 * (s % 2):1024 * (s % 2) + 1024],
                in_=x_d[512 * ss + 128 * s:512 * ss + 128 * s + 128, :]),
                wait=w, inc=("LX", 16))

        def phaseA_part(ss, half):
            if half == 0:
                w0 = [("DV", M["resid"][3])] if ss > 0 else []
                ldx_op(ss, 0, w0)
                ldx_op(ss, 1, w0)
                for s in (0, 1):
                    ln_stats(s, xs[:, 1024 * (s % 2):1024 * (s % 2) + 1024],
                             [("LX", p.cnt["LX"])])
            else:
                for s in (2, 3):
                    ldx_op(ss, s, [("DV", M["ln_red"][s - 1]), ("AC", M["ln_sq"][s - 1])])
                    ln_stats(s, xs[:, 1024 * (s % 2):1024 * (s % 2) + 1024],
                             [("LX", p.cnt["LX"])])

        phaseA_part(0, 0)
        for s in (0, 1):
            ln_trans(s, xT)
        phaseA_part(0, 1)
        for s in (2, 3):
            ln_trans(s, xT)
        M["ac_xT"] = p.cnt["AC"]

        for ss in range(NSS):
            if _ph < 1:
                break
            M["peB0"] = p.cnt["PE"]
            pl_B = p.cnt["PL"]
            grp = 0
            ev_val = [0, 0, 0]
            ev_eng = ["DV", "AC", "DV"]
            for wsb, tgt in ((wq_sb, QTr), (wk_sb, KTr), (wvo_sb, VTr)):
                if tgt is VTr:
                    p.op("sync", lambda sy: sy.dma_start(
                        out=wvo_sb[:, :], in_=wvo_d[0]),
                        wait=([("PE", M["wo_last"])] if M["wo_last"] else []),
                        inc=("LV", 16))
                for F in range(8):
                    bank = (pA, pB, pD)[grp % 3]
                    w0 = [("AC", M["ac_xT"])]
                    if tgt is VTr and F == 0:
                        w0.append(("LV", p.cnt["LV"]))
                    if grp == 2 and ss > 0:
                        w0.append(("DV", M["epi1"][3]))
                    if ev_val[grp % 3]:
                        w0.append((ev_eng[grp % 3], ev_val[grp % 3]))
                    pes = 0
                    for dc in range(8):
                        pes = p.op("tensor", lambda t, wsb=wsb, F=F, dc=dc, bank=bank: t.matmul(
                            bank[:, :], wsb[:, 1024 * dc + 128 * F:1024 * dc + 128 * F + 128],
                            xT[:, 512 * dc:512 * dc + 512],
                            start=(dc == 0), stop=(dc == 7)),
                            wait=(w0 if dc == 0 else ()),
                            inc=(("PE", 1) if dc == 7 else None))
                    eng = "vector" if grp % 2 == 0 else "scalar"
                    last = 0
                    for pr in range(2):
                        r = 2 * F + pr
                        dst = tgt[:, :, r, :]
                        wev = [("PE", pes)] if pr == 0 else []
                        if grp == 0 and F == 0 and pr == 0:
                            wev.append(("PL", pl_B))
                        if eng == "vector":
                            last = p.op("vector", lambda v, dst=dst, pr=pr, bank=bank: v.tensor_copy(
                                out=dst, in_=bank[64 * pr:64 * pr + 64, :]),
                                wait=wev, inc=("DV", 1))
                        else:
                            last = p.op("scalar", lambda sc, dst=dst, pr=pr, bank=bank: sc.copy(
                                out=dst, in_=bank[64 * pr:64 * pr + 64, :]),
                                wait=([("PE", pes)] if pr == 0 else ()), inc=("AC", 1))
                    ev_val[grp % 3] = last
                    ev_eng[grp % 3] = "DV" if eng == "vector" else "AC"
                    grp += 1
            p.op("sync", lambda sy: sy.dma_start(
                out=wvo_sb[:, :], in_=wvo_d[1]),
                wait=[("PE", p.cnt["PE"])], inc=("LV", 16))
            evB_dv = p.cnt["DV"]
            evB_ac = p.cnt["AC"]
            dv_f = p.cnt["DV"]
            st_B = p.cnt["ST"]

            _phC = _ph >= 3

            def b2_sq(s):
                B2 = 2048 * s
                wmin = [("DV", M["ek_last"])]
                if s == 0:
                    wmin += [("DV", evB_dv), ("AC", evB_ac)]
                m1 = p.op("vector", lambda v, B2=B2: v.tensor_scalar(
                    out=zsc[:, 0:2048], in0=QT[:, B2:B2 + 2048],
                    scalar1=0.0, scalar2=None, op0=ALU.min), wait=wmin, inc=("DV", 1))
                e1 = p.op("scalar", lambda sc: sc.activation(
                    out=zsc[:, 0:2048], in_=zsc[:, 0:2048], func=AF.Exp),
                    wait=[("DV", m1)], inc=("AC", 1))
                dsq = p.op("vector", lambda v, B2=B2: v.scalar_tensor_tensor(
                    out=SQ[:, B2:B2 + 2048], in0=QT[:, B2:B2 + 2048], scalar=0.0,
                    in1=zsc[:, 0:2048], op0=ALU.max, op1=ALU.add),
                    wait=[("AC", e1), ("PE", M["peB0"])], inc=("DV", 1))
                M["sq_s"][s] = dsq

            def b2_ek(s):
                B2 = 2048 * s
                m2 = p.op("vector", lambda v, B2=B2: v.tensor_scalar(
                    out=zsc[:, 0:2048], in0=KT[:, B2:B2 + 2048],
                    scalar1=0.0, scalar2=None, op0=ALU.min),
                    wait=[("DV", M["sq_s"][s])], inc=("DV", 1))
                e2 = p.op("scalar", lambda sc: sc.activation(
                    out=zsc[:, 0:2048], in_=zsc[:, 0:2048], func=AF.Exp),
                    wait=[("DV", m2)], inc=("AC", 1))
                for h in range(4):
                    w = [("AC", e2), ("AC", M["ln_sq"][2])] if h == 0 else []
                    M["ek_last"] = p.op("vector", lambda v, s=s, h=h: v.scalar_tensor_tensor(
                        out=zsc[:, 512 * h:512 * h + 512],
                        in0=KT[:, (4 * s + h) * 512:(4 * s + h) * 512 + 512],
                        scalar=0.0, in1=zsc[:, 512 * h:512 * h + 512],
                        op0=ALU.max, op1=ALU.add, accum_out=znt[:, h:h + 1]),
                        wait=w, inc=("DV", 1))
                p.drain("vector")
                p.op("vector", lambda v: v.tensor_add(out=zc_pref[:, :], in0=zc_pref[:, :],
                                                      in1=znt[:, 0:4]))
                p.drain("vector")
                newton(zrec[:, 4 * s:4 * s + 4], zc_pref[:, 0:4], zaccs[:, 0:4])
                M["gz_s"][s] = p.cnt["DV"]

            def stage1(idx):
                s, ho = divmod(idx, 4)
                par = idx % 2
                Bq = (4 * s + ho) * 512
                ncs = 4
                if s2p is not S2ALL and "sc1" in s2p:
                    ncs = 1
                elif s2p is not S2ALL and "sc2" in s2p:
                    ncs = 2
                elif s2p is not S2ALL and "sc3" in s2p:
                    ncs = 3
                for c in (range(ncs) if ("sc" in s2p or "sc1" in s2p or "sc2" in s2p
                                         or "sc3" in s2p or s2p is S2ALL) else []):
                    bank = (pS0, pS1)[(4 * idx + c) % 2]
                    w = [("AC", M["expq"][-2])]
                    if c == 0 and "nw" not in s2p:
                        w.append(("DV", dv_f))
                    if idx == 0:
                        w += [("DV", dv_f), ("AC", evB_ac)]
                    spe = p.op("tensor", lambda t, c=c, Bq=Bq, bank=bank: t.matmul(
                        bank[:, :], KT[:, Bq + 128 * c:Bq + 128 * c + 128],
                        QT[:, Bq:Bq + 512], start=True, stop=True),
                        wait=w, inc=("PE", 1))
                    if "sx" not in s2p and s2p is not S2ALL:
                        M["expq"].append(0)
                        M["pexp_%d_%d" % (idx, c)] = 0
                        continue
                    w2 = [("PE", spe)]
                    if c == 0:
                        w2.append(("PE", M.get("attd_%d" % (idx - 2), 0)))
                    a = p.op("scalar", lambda sc, c=c, par=par, bank=bank: sc.activation(
                        out=Pt[:, 2048 * par + 512 * c:2048 * par + 512 * c + 512],
                        in_=bank[:, :], func=AF.Exp, scale=0.125), wait=w2, inc=("AC", 1))
                    M["expq"].append(a)
                    M["pexp_%d_%d" % (idx, c)] = a
                M["expq"] = M["expq"][-8:]

            def stage2a(idx):
                s, ho = divmod(idx, 4)
                par = idx % 2
                Bq = (4 * s + ho) * 512
                abank = pAT if par == 0 else pD
                qtr = 0
                for c in (range(4) if "qvtr" in s2p else []):
                    w = ()
                    if c == 0:
                        w = [("DV", max(M["sqev"], M["vev"], M["sq_s"][s]))]
                        if idx == 0:
                            w += [("DV", evB_dv), ("AC", evB_ac)]
                    qtr = p.op("tensor", lambda t, c=c, Bq=Bq: t.transpose(
                        pTR[:, 128 * c:128 * c + 128],
                        SQV[:, Bq + 128 * c:Bq + 128 * c + 128],
                        ident[:, :]), wait=w, inc=(("PE", 1) if c == 3 else None))
                sqv_src = pTR[:, 0:512].rearrange("p (k c) -> p k c", c=128)
                if "evac" not in s2p:
                    return
                M["sqev"] = p.op("vector", lambda v, par=par, sqv_src=sqv_src: v.tensor_copy(
                    out=sqtm[:, 256 * par:256 * par + 256].rearrange("p (k c) -> p k c", c=64),
                    in_=sqv_src[:, :, 0:64]),
                    wait=[("PE", qtr), ("PE", M.get("mem_%d" % (idx - 2), 0))], inc=("DV", 1))
                vdst = vaug[:, 260 * par:260 * par + 260].rearrange(
                    "p (k c) -> p k c", c=65)[:, :, 0:64]
                M["vev"] = p.op("vector", lambda v, vdst=vdst, sqv_src=sqv_src: v.tensor_copy(
                    out=vdst, in_=sqv_src[:, :, 64:128]),
                    wait=[("PE", max(M.get("attd_%d" % (idx - 2), 0),
                                     M.get("mem_%d" % (idx - 2), 0)))], inc=("DV", 1))
                M["vev_%d" % idx] = M["vev"]
                if "mem" not in s2p:
                    return
                for c in range(4):
                    w = [("DV", max(M["sqev"], M["vev"], M["memadd"]))]
                    p.op("tensor", lambda t, c=c, par=par: t.matmul(
                        pA[64:128, 0:64], sqtm[:, 256 * par + 64 * c:256 * par + 64 * c + 64],
                        vaug[:, 260 * par + 65 * c:260 * par + 65 * c + 64],
                        start=(c == 0), stop=(c == 3)),
                        wait=(w if c == 0 else ()), inc=(("PE", 1) if c == 3 else None))
                M["mem_%d" % idx] = p.cnt["PE"]
                if "attd" not in s2p:
                    return
                for c in range(4):
                    w = [("AC", M["pexp_%d_%d" % (idx, c)]), ("DV", M["vev_%d" % idx])]
                    if c == 0:
                        w.append(("DV", max(M["u1_%d" % par], dv_f)))
                    p.op("tensor", lambda t, c=c, par=par, abank=abank: t.matmul(
                        abank[0:65, :], vaug[:, 260 * par + 65 * c:260 * par + 65 * c + 65],
                        Pt[:, 2048 * par + 512 * c:2048 * par + 512 * c + 512],
                        start=(c == 0), stop=(c == 3)),
                        wait=w, inc=(("PE", 1) if c == 3 else None))
                M["attd_%d" % idx] = p.cnt["PE"]
                if "membuf" not in s2p:
                    return
                M["memadd"] = p.op("vector", lambda v, ho=ho: v.tensor_add(
                    out=mem_pref[:, 64 * ho:64 * ho + 64],
                    in0=mem_pref[:, 64 * ho:64 * ho + 64], in1=pA[64:128, 0:64]),
                    wait=[("PE", M["mem_%d" % idx])], inc=("DV", 1))
                mb = p.op("gpsimd", lambda g, ho=ho, par=par: g.tensor_copy(
                    out=membuf[:, 65 * par:65 * par + 64],
                    in_=mem_pref[:, 64 * ho:64 * ho + 64]),
                    wait=[("DV", M["memadd"]),
                          ("PE", M.get("num_%d" % (idx - 2), 0))], inc=("PL", 1))
                if "num" not in s2p:
                    return
                penum = p.op("tensor", lambda t, par=par, Bq=Bq: t.matmul(
                    pC[0:65, :], membuf[:, 65 * par:65 * par + 65],
                    SQ[:, Bq:Bq + 512], start=True, stop=True),
                    wait=[("PL", mb), ("DV", max(M["numev"], dv_f))], inc=("PE", 1))
                M["num_%d" % idx] = penum
                if "rr" not in s2p:
                    return
                p.op("vector", lambda v, abank=abank, par=par: v.tensor_copy(
                    out=rr_in[0:1, 1024 * par:1024 * par + 512], in_=abank[64:65, :]),
                    wait=[("PE", M["attd_%d" % idx]), ("AC", M["lnr_%d" % par])],
                    inc=("DV", 1))
                M["rrB"] = p.op("vector", lambda v, par=par: v.tensor_copy(
                    out=rr_in[0:1, 1024 * par + 512:1024 * par + 1024], in_=pC[64:65, :]),
                    wait=[("PE", penum)], inc=("DV", 1))
                M["numev"] = p.op("vector", lambda v, par=par: v.tensor_copy(
                    out=numsb[:, 512 * par:512 * par + 512], in_=pC[0:64, :]),
                    inc=("DV", 1))
                M["rrB_%d" % idx] = M["rrB"]
                M["numev_%d" % idx] = M["numev"]

            def stage2b(idx):
                s, ho = divmod(idx, 4)
                par = idx % 2
                abank = pAT if par == 0 else pD
                if "rr" not in s2p:
                    return
                M["lnr_%d" % par] = p.op("scalar", lambda sc, par=par: sc.activation(
                    out=rr_in[0:1, 1024 * par:1024 * par + 1024],
                    in_=rr_in[0:1, 1024 * par:1024 * par + 1024], func=AF.Ln),
                    wait=[("DV", M["rrB_%d" % idx])], inc=("AC", 1))
                p.drain("scalar")
                we = [("PE", M["bcpe"])]
                if idx == 0:
                    we.append(("DV", dv_f))
                M["expR"] = p.op("scalar", lambda sc, par=par: sc.activation(
                    out=rrow[0:1, 0:1024], in_=rr_in[0:1, 1024 * par:1024 * par + 1024],
                    func=AF.Exp, scale=-1.0), wait=we, inc=("AC", 1))
                if "bc" not in s2p:
                    return
                w = [("AC", M["expR"]), ("DV", M["bcbe"])]
                if idx == 0:
                    w += [("DV", dv_f), ("AC", evB_ac)]
                p.op("tensor", lambda t, ho=ho: t.matmul(
                    pB[:, :], gblkZ[0:1, 128 * ho:128 * ho + 128], rrow[0:1, 512:1024],
                    start=True, stop=False), wait=w, inc=("PE", 1))
                M["bcpe"] = p.op("tensor", lambda t, ho=ho: t.matmul(
                    pB[:, :], gblkZ[0:1, 512 + 128 * ho:512 + 128 * ho + 128],
                    rrow[0:1, 0:512], start=False, stop=True), inc=("PE", 1))
                wb = [("PE", M["bcpe"])]
                if idx == 0:
                    wb.append(("ST", st_B))
                M["bcbe"] = p.op("vector", lambda v, par=par: v.tensor_copy(
                    out=bcB[:, 512 * par:512 * par + 512], in_=pB[:, :]),
                    wait=wb, inc=("DV", 1))
                if "u" not in s2p:
                    return
                M["u1_%d" % par] = p.op("vector", lambda v, par=par, abank=abank: v.tensor_mul(
                    out=uA[:, 512 * par:512 * par + 512], in0=abank[0:64, :],
                    in1=bcB[64:128, 512 * par:512 * par + 512]),
                    wait=[("PL", M.get("fin_%d" % (idx - 2), 0))], inc=("DV", 1))
                M["u2"] = p.op("vector", lambda v, par=par, s=s, ho=ho: v.scalar_tensor_tensor(
                    out=uB[:, 512 * par:512 * par + 512],
                    in0=numsb[:, 512 * par:512 * par + 512],
                    scalar=zrec[:, 4 * s + ho:4 * s + ho + 1],
                    in1=bcB[0:64, 512 * par:512 * par + 512],
                    op0=ALU.mult, op1=ALU.mult), inc=("DV", 1))
                if "fin" not in s2p:
                    return
                attv = ATT[:, :].rearrange("p (G c) -> p G c", c=512)
                fin = 0
                for pr in range(2):
                    uBv = uB[:, 512 * par:512 * par + 512].rearrange(
                        "j (G two m) -> j G two m", two=2, m=32)[:, :, pr, :]
                    uAv = uA[:, 512 * par:512 * par + 512].rearrange(
                        "j (G two m) -> j G two m", two=2, m=32)[:, :, pr, :]
                    dst = attv[64 * pr:64 * pr + 64, :,
                               128 * s + 32 * ho:128 * s + 32 * ho + 32]
                    w = [("DV", M["u2"]), ("DV", M["gz_s"][s])] if pr == 0 else ()
                    fin = p.op("gpsimd", lambda g, uBv=uBv, uAv=uAv, dst=dst:
                               g.tensor_tensor(out=dst, in0=uBv, in1=uAv, op=ALU.add),
                               wait=w, inc=("PL", 1))
                M["fin_%d" % idx] = fin

            for idx in (range(min(18, c_lim + 2)) if _phC else []):
                if idx < min(16, c_lim):
                    if "b2" in s2p or s2p is S2ALL:
                        if idx == 0:
                            b2_sq(0)
                            b2_ek(0)
                        elif idx % 4 == 2 and idx < 14:
                            b2_sq(idx // 4 + 1)
                        elif idx % 4 == 3 and idx < 15:
                            b2_ek(idx // 4 + 1)
                    stage1(idx)
                if idx >= 1 and idx - 1 < min(16, c_lim):
                    stage2a(idx - 1)
                if idx >= 2:
                    stage2b(idx - 2)

            if _ph < 4:
                continue
            for s in range(2):
                w = [("DV", M["ln_ap"][3]), ("AC", M["ln_sq"][3])]
                w.append(("DV", M["ek_last"]))
                p.op("sync", lambda sy, ss=ss, s=s: sy.dma_start(
                    out=xs[:, 1024 * (s % 2):1024 * (s % 2) + 1024],
                    in_=x_d[512 * ss + 128 * s:512 * ss + 128 * s + 128, :]),
                    wait=w, inc=("LX", 16))
            for s in range(4):
                for Dh in range(2):
                    k = 2 * s + Dh
                    bank = pA if k % 2 == 0 else pB
                    w = [("PL", M["fin_%d" % (4 * s + 3)])]
                    if k == 0:
                        w.append(("DV", M["memadd"]))
                        w.append(("LV", p.cnt["LV"]))
                    if k == 1:
                        w.append(("DV", M["bcbe"]))
                    if k >= 2:
                        w.append(("DV", M["woev_%d" % (k - 2)]))
                    pes = 0
                    for G in range(8):
                        pes = p.op("tensor", lambda t, s=s, Dh=Dh, G=G, bank=bank: t.matmul(
                            bank[:, :], ATT[:, (4 * G + s) * 128:(4 * G + s) * 128 + 128],
                            wvo_sb[:, 1024 * G + 512 * Dh:1024 * G + 512 * Dh + 512],
                            start=(G == 0), stop=(G == 7)),
                            wait=(w if G == 0 else ()), inc=(("PE", 1) if G == 7 else None))
                    M["wo_last"] = pes
                    dvr = p.op("vector", lambda v, s=s, Dh=Dh, bank=bank: v.tensor_add(
                        out=y1[:, 1024 * s + 512 * Dh:1024 * s + 512 * Dh + 512],
                        in0=bank[:, :],
                        in1=xs[:, 1024 * (s % 2) + 512 * Dh:1024 * (s % 2) + 512 * Dh + 512]),
                        wait=([("PE", pes), ("LX", p.cnt["LX"])]
                              + ([("ST", p.cnt["ST"])] if k == 0 else [])),
                        inc=("DV", 1))
                    M["woev_%d" % k] = dvr
                    if Dh == 1:
                        M["resid"][s] = dvr
                        if s < 2:
                            p.op("sync", lambda sy, ss=ss, s=s: sy.dma_start(
                                out=xs[:, 1024 * (s % 2):1024 * (s % 2) + 1024],
                                in_=x_d[512 * ss + 128 * (s + 2):512 * ss + 128 * (s + 2) + 128, :]),
                                wait=[("DV", dvr)], inc=("LX", 16))
                        if _ph >= 5:
                            if s >= 2:
                                ln_trans(s - 2, yT)
                            ln_stats(s, y1[:, 1024 * s:1024 * s + 1024],
                                     [("DV", M["resid"][s])])
            p.drain("vector")

            if _ph < 5:
                continue
            for s in (2, 3):
                ln_trans(s, yT)
            M["ac_yT"] = p.cnt["AC"]
            for s in range(4):
                w = [("AC", M["ac_yT"])] if s == 0 else ()
                p.op("vector", lambda v, s=s: v.tensor_add(
                    out=y1[:, 1024 * s:1024 * s + 1024], in0=y1[:, 1024 * s:1024 * s + 1024],
                    in1=b2bc[:, :]), wait=w, inc=(("DV", 1) if s == 3 else None))
            dv_y1b = p.cnt["DV"]
            p.drain("vector")
            if ss + 1 < NSS:
                phaseA_part(ss + 1, 0)

            if _ph < 6:
                continue
            banks = [pS0, pS1, pAT, pD]
            prevh1 = max(M["h1"][0], M["h1"][1])
            prevl2 = M["fstop"][3]

            W1CH = ("LWa", "LWb", "LWc", "LWd")
            W2CH = ("L2a", "L2b", "L2c", "L2d")

            def ld_w1(F, gate):
                return p.op("sync", lambda sy, F=F: sy.dma_start(
                    out=w1s[:, 1024 * (F % 4):1024 * (F % 4) + 1024], in_=w1t_d[F]),
                    wait=gate, inc=(W1CH[F % 4], 16))

            def ld_w2(F, half, gate):
                return p.op("gpsimd", lambda g, F=F, half=half: g.dma_start(
                    out=w2s[:, 512 * (F % 4):512 * (F % 4) + 512],
                    in_=w2t_d[F][:, 512 * half:512 * half + 512]),
                    wait=gate, inc=(W2CH[F % 4], 16))

            for FF in range(4):
                ld_w1(FF, [("PE", prevh1)] if prevh1 else [])
                ld_w2(FF, 0, [("PE", prevl2)] if prevl2 else [])
            for F in range(33):
                if F == 16 and ss + 1 < NSS:
                    ln_trans(0, xT)
                    ln_trans(1, xT)
                    phaseA_part(ss + 1, 1)
                if F < 32:
                    par = F % 2
                    slot = F % 4
                    bank = pA if par == 0 else pB
                    w = [(W1CH[slot], p.cnt[W1CH[slot]]),
                         ("AC", max(M["ac_yT"], M["gelu"][par]))]
                    if F == 0:
                        w.append(("DV", dv_y1b))
                    pes = 0
                    for dc in range(8):
                        pes = p.op("tensor", lambda t, dc=dc, slot=slot, bank=bank: t.matmul(
                            bank[:, :], w1s[:, 1024 * slot + 128 * dc:1024 * slot + 128 * dc + 128],
                            yT[:, 512 * dc:512 * dc + 512], start=(dc == 0), stop=(dc == 7)),
                            wait=(w if dc == 0 else ()),
                            inc=(("PE", 1) if dc == 7 else None))
                    M["h1"][par] = pes
                    M["gelu"][par] = p.op("scalar", lambda sc, F=F, bank=bank: sc.activation(
                        out=HH[:, 512 * F:512 * F + 512], in_=bank[:, :], func=AF.Gelu,
                        scale=0.5, bias=b1c[:, F:F + 1]),
                        wait=[("PE", pes)], inc=("AC", 1))
                if F >= 1:
                    Fp = F - 1
                    parp = Fp % 2
                    slotp = Fp % 4
                    for tc in range(4):
                        w = [("AC", M["gelu"][parp])]
                        if tc == 0:
                            w.append((W2CH[slotp], p.cnt[W2CH[slotp]]))
                        if Fp == 0:
                            w.append(("DV", dv_y1b))
                            if ss > 0:
                                w.append(("DV", M["epi1"][tc]))
                        pes = p.op("tensor", lambda t, Fp=Fp, tc=tc, slotp=slotp, bk=banks[tc]: t.matmul(
                            bk[:, :], HH[:, 512 * Fp + 128 * tc:512 * Fp + 128 * tc + 128],
                            w2s[:, 512 * slotp:512 * slotp + 512],
                            start=(Fp == 0), stop=(Fp == 31)),
                            wait=(w if (tc == 0 or Fp == 0) else ()), inc=("PE", 1))
                        if tc == 3:
                            M["ffn_o3"] = pes
                        if Fp == 31:
                            M["fstop"][tc] = pes
                    if Fp + 4 < 32:
                        ld_w2(Fp + 4, 0, [("PE", M["ffn_o3"])])
                if F < 32 and F + 4 < 32:
                    ld_w1(F + 4, [("PE", M["h1"][F % 2])])
            for tc in range(4):
                w = [("PE", M["fstop"][tc])]
                dve = p.op("vector", lambda v, tc=tc, bk=banks[tc]: v.scalar_tensor_tensor(
                    out=y1[:, 1024 * tc:1024 * tc + 512], in0=bk[:, :], scalar=0.5,
                    in1=y1[:, 1024 * tc:1024 * tc + 512], op0=ALU.mult, op1=ALU.add),
                    wait=w, inc=("DV", 1))
                M["epi0"][tc] = dve
                p.op("gpsimd", lambda g, ss=ss, tc=tc: g.dma_start(
                    out=y_d[512 * ss + 128 * tc:512 * ss + 128 * tc + 128, 0:512],
                    in_=y1[:, 1024 * tc:1024 * tc + 512]),
                    wait=[("DV", dve)], inc=("ST", 16))
            prevcar = M["fstop"][3]
            for FF in range(4):
                ld_w2(FF, 1, [("PE", prevcar)])
            for F in range(32):
                slot = F % 4
                for tc in range(4):
                    w = []
                    if tc == 0:
                        w.append((W2CH[slot], p.cnt[W2CH[slot]]))
                    if F == 0:
                        w.append(("DV", M["epi0"][tc]))
                    pes = p.op("tensor", lambda t, F=F, tc=tc, slot=slot, bk=banks[tc]: t.matmul(
                        bk[:, :], HH[:, 512 * F + 128 * tc:512 * F + 128 * tc + 128],
                        w2s[:, 512 * slot:512 * slot + 512],
                        start=(F == 0), stop=(F == 31)),
                        wait=(w if (tc == 0 or F == 0) else ()), inc=("PE", 1))
                    if tc == 3:
                        M["ffn_o3"] = pes
                    if F == 31:
                        M["fstop"][tc] = pes
                if F + 4 < 32:
                    ld_w2(F + 4, 1, [("PE", M["ffn_o3"])])
            for tc in range(4):
                w = [("PE", M["fstop"][tc])]
                dve = p.op("vector", lambda v, tc=tc, bk=banks[tc]: v.scalar_tensor_tensor(
                    out=y1[:, 1024 * tc + 512:1024 * tc + 1024], in0=bk[:, :], scalar=0.5,
                    in1=y1[:, 1024 * tc + 512:1024 * tc + 1024], op0=ALU.mult, op1=ALU.add),
                    wait=w, inc=("DV", 1))
                M["epi1"][tc] = dve
                p.op("gpsimd", lambda g, ss=ss, tc=tc: g.dma_start(
                    out=y_d[512 * ss + 128 * tc:512 * ss + 128 * tc + 128, 512:1024],
                    in_=y1[:, 1024 * tc + 512:1024 * tc + 1024]),
                    wait=[("DV", dve)], inc=("ST", 16))
            if ss + 1 < NSS:
                ln_trans(2, xT)
                ln_trans(3, xT)
                M["ac_xT"] = p.cnt["AC"]

        p.wait("gpsimd", "ST", p.cnt["ST"])
        p.emit(block, sems)
    return nc


_NC_CACHE = [None]


def kernel(L0_attn_x, L0_attn_Wq, L0_attn_Wk, L0_attn_Wv, L0_attn_Wo, L0_attn_betas,
           L0_ffn_W1, L0_ffn_b1, L0_ffn_W2, L0_ffn_b2):
    x = np.ascontiguousarray(np.asarray(L0_attn_x, dtype=np.float32))
    wq = np.ascontiguousarray(np.asarray(L0_attn_Wq, dtype=np.float32))
    wk = np.ascontiguousarray(np.asarray(L0_attn_Wk, dtype=np.float32))
    wv = np.ascontiguousarray(np.asarray(L0_attn_Wv, dtype=np.float32))
    wo = np.ascontiguousarray(np.asarray(L0_attn_Wo, dtype=np.float32))
    bet = np.ascontiguousarray(np.asarray(L0_attn_betas, dtype=np.float32).reshape(16, 64))
    w1 = np.ascontiguousarray(np.asarray(L0_ffn_W1, dtype=np.float32))
    b1 = np.ascontiguousarray(np.asarray(L0_ffn_b1, dtype=np.float32))
    w2 = np.ascontiguousarray(np.asarray(L0_ffn_W2, dtype=np.float32))
    b2 = np.ascontiguousarray(np.asarray(L0_ffn_b2, dtype=np.float32))

    if _NC_CACHE[0] is None:
        _NC_CACHE[0] = build_program()
    nc = _NC_CACHE[0]

    in_maps = []
    for core in range(8):
        b, q = divmod(core, 4)
        xc = np.ascontiguousarray(
            x[b].reshape(32, 4, 128, 1024)[:, q].reshape(4096, 1024))
        in_maps.append({
            "x": xc, "wq": wq, "wk": wk, "wv": wv, "wo": wo,
            "betas": np.ascontiguousarray(bet[4 * q:4 * q + 4]),
            "w1": w1, "b1": b1, "w2": w2, "b2": b2,
        })
    res = run_bass_kernel_spmd(nc, in_maps, core_ids=list(range(8)))
    out = np.empty((2, 32, 4, 128, 1024), dtype=np.float32)
    for core in range(8):
        b, q = divmod(core, 4)
        out[b, :, q] = res.results[core]["y"].reshape(32, 128, 1024)
    return out.reshape(2, 16384, 1024)
